# version 10
# baseline (speedup 1.0000x reference)
import numpy as np
from contextlib import ExitStack

import concourse.bass as bass
import concourse.mybir as mybir
from concourse.bass_utils import run_bass_kernel_spmd

F32 = mybir.dt.float32
BF16 = mybir.dt.bfloat16
U32 = mybir.dt.uint32
AF = mybir.ActivationFunctionType
ALU = mybir.AluOpType
AX = mybir.AxisListType
IOffs = bass.IndirectOffsetOnAxis

D = 2048
KT = 16
CH = 1024
TT = 8
NCH = 4
ITL = 11
SITL = 22
C = 384
NSB = 24
BIG = 4096.0
EPS = 1e-5
DENOM = float(np.sqrt(2048.0) / 8192.0)

_NCH = NCH
_NEB = 16

ENGS = ("sync", "gpsimd", "tensor", "scalar", "vector")

SEM_NAMES = (
    ["p", "v", "a", "cc"]
    + ["xld0", "xld1", "xnb0", "xnb1", "xg0", "xg1", "xg2", "xg3"]
    + ["yev0", "yev1"]
    + ["yc0", "yc1", "yc2", "yc3", "xos0", "xos1"]
    + ["wgu0", "wgu1", "wgu2", "wdw0", "wdw1", "wdw2", "dsm"]
    + ["wgc0", "wgc1", "wgc2", "wdc0", "wdc1", "wdc2"]
    + ["din0", "din1", "wst0", "wst1"]
    + ["wq0", "wq1", "wk0", "wk1", "wv0", "wv1", "wo0", "wo1"]
    + ["qst", "kvst0", "kvst1", "qld", "kvld", "kvld2", "ccin", "ccout", "misc"]
    + ["xin0", "xin1", "xin2", "xin3", "xst0", "xst1", "xst2", "xst3"]
)


class Walk:
    def __init__(self, name, eng, sems):
        self.name = name
        self.E = eng
        self.sems = sems
        self.n = dict.fromkeys(sems, 0)
        self.lastw = {}

    def me(self, nm):
        return self.name == nm

    def fire(self, ins, s, k=1):
        self.n[s] += k
        if ins is not None:
            ins.then_inc(self.sems[s], k)
        return self.n[s]

    def wait(self, who, s, val):
        if val <= 0 or self.name != who:
            return
        if self.lastw.get(s, 0) >= val:
            return
        self.lastw[s] = val
        self.E.wait_ge(self.sems[s], val)


def build_nc():
    nc = bass.Bass("TRN2", target_bir_lowering=False)

    x0 = nc.dram_tensor("x0", [4096, D], F32, kind="ExternalInput").ap()
    Wq = nc.dram_tensor("Wq", [D, D], F32, kind="ExternalInput").ap()
    Wk = nc.dram_tensor("Wk", [D, D], F32, kind="ExternalInput").ap()
    Wv = nc.dram_tensor("Wv", [D, D], F32, kind="ExternalInput").ap()
    Wo = nc.dram_tensor("Wo", [D, D], F32, kind="ExternalInput").ap()
    router = nc.dram_tensor("router_w", [8, D], F32, kind="ExternalInput").ap()
    gate_w = nc.dram_tensor("gate_w", [8, D, 1408], F32, kind="ExternalInput").ap()
    up_w = nc.dram_tensor("up_w", [8, D, 1408], F32, kind="ExternalInput").ap()
    down_w = nc.dram_tensor("down_w", [8, 1408, D], F32, kind="ExternalInput").ap()
    s_gate_w = nc.dram_tensor("s_gate_w", [D, 2816], F32, kind="ExternalInput").ap()
    s_gate_b = nc.dram_tensor("s_gate_b", [2816], F32, kind="ExternalInput").ap()
    s_up_w = nc.dram_tensor("s_up_w", [D, 2816], F32, kind="ExternalInput").ap()
    s_up_b = nc.dram_tensor("s_up_b", [2816], F32, kind="ExternalInput").ap()
    s_down_w = nc.dram_tensor("s_down_w", [2816, D], F32, kind="ExternalInput").ap()
    s_down_b = nc.dram_tensor("s_down_b", [D], F32, kind="ExternalInput").ap()
    idf = nc.dram_tensor("idf", [128, 128], F32, kind="ExternalInput").ap()
    idfb = nc.dram_tensor("idfb", [128, 128], BF16, kind="ExternalInput").ap()
    triU = nc.dram_tensor("triU", [128, 128], F32, kind="ExternalInput").ap()
    iotaF = nc.dram_tensor("iotaF", [128, 128], F32, kind="ExternalInput").ap()
    iota2 = nc.dram_tensor("iota2", [128, 2 * TT], BF16, kind="ExternalInput").ap()
    eCbd = nc.dram_tensor("eCb", [128, 8], F32, kind="ExternalInput").ap()
    maskv = nc.dram_tensor("maskv", [128, 1], F32, kind="ExternalInput").ap()
    out_e = nc.dram_tensor("out", [4096, D], F32, kind="ExternalOutput").ap()

    x1d = nc.dram_tensor("x1d", [4096, D], F32, kind="Internal").ap()
    x2d = nc.dram_tensor("x2d", [4096, D], F32, kind="Internal").ap()
    xnd = [nc.dram_tensor("xnd%d" % c_, [CH, D], BF16, kind="Internal").ap()
           for c_ in range(NCH)]
    yd = [nc.dram_tensor("yd%d" % c_, [8 * C + CH, D], BF16, kind="Internal").ap()
          for c_ in range(NCH)]
    qTd = nc.dram_tensor("qTd", [NCH, 128, KT * CH], BF16, kind="Internal").ap()
    kvtd = nc.dram_tensor("kvtd", [KT, 128, 4096], F32, kind="Internal").ap()
    wtd = nc.dram_tensor("wtd", [3, 128, 16, 16, 128], BF16, kind="Internal").ap()
    gwc = nc.dram_tensor("gwc", [8, ITL, 128, KT * 128], BF16, kind="Internal").ap()
    uwc = nc.dram_tensor("uwc", [8, ITL, 128, KT * 128], BF16, kind="Internal").ap()
    dwc = nc.dram_tensor("dwc", [8, 4, 128, ITL * 512], BF16, kind="Internal").ap()
    sgwc = nc.dram_tensor("sgwc", [SITL, 128, KT * 128], BF16, kind="Internal").ap()
    suwc = nc.dram_tensor("suwc", [SITL, 128, KT * 128], BF16, kind="Internal").ap()
    sdwc = nc.dram_tensor("sdwc", [4, 2, 128, ITL * 512], BF16, kind="Internal").ap()
    wod = nc.dram_tensor("wod", [128, 16, 16, 128], BF16, kind="Internal").ap()
    cc_in = nc.dram_tensor("cc_in", [128, 16], F32, kind="Internal").ap()
    cc_out = nc.dram_tensor("cc_out", [256, 16], F32, kind="Internal").ap()

    st = ExitStack()
    e = st.enter_context

    stat = e(nc.sbuf_tensor("stat", [128, 64], F32))
    mx = e(nc.sbuf_tensor("mx", [128, TT * 8], F32))
    dd = e(nc.sbuf_tensor("dd", [128, TT * 2], F32))
    ww = e(nc.sbuf_tensor("ww", [128, TT * 2], F32))
    w1p = e(nc.sbuf_tensor("w1p", [128, TT * 2], F32))
    logit = e(nc.sbuf_tensor("logit", [128, 8], F32))
    msk1 = e(nc.sbuf_tensor("msk1", [128, TT * 8], F32))
    msk2 = e(nc.sbuf_tensor("msk2", [128, TT * 8], F32))
    mskS = e(nc.sbuf_tensor("mskS", [128, TT * 8], F32))
    r_e = e(nc.sbuf_tensor("r_e", [128, TT * 8], F32))
    dest = e(nc.sbuf_tensor("dest", [128, TT * 8], F32))
    dgt = e(nc.sbuf_tensor("dgt", [128, 8], F32))
    sgt = e(nc.sbuf_tensor("sgt", [128, 16], F32))
    amask = e(nc.sbuf_tensor("amask", [128, 8], F32))
    dmm = e(nc.sbuf_tensor("dmm", [128, 8], F32))
    dsh = e(nc.sbuf_tensor("dsh", [128, NSB], F32))
    slotsel = e(nc.sbuf_tensor("slotsel", [128, TT * 2], F32))
    slotu = e(nc.sbuf_tensor("slotu", [128, TT * 2], U32))
    gtokf = e(nc.sbuf_tensor("gtokf", [128, NSB], F32))
    gtmp = e(nc.sbuf_tensor("gtmp", [128, 2 * NSB], F32))
    idxu = e(nc.sbuf_tensor("idxu", [128, NSB], U32))
    totr = e(nc.sbuf_tensor("totr", [1, TT * 8], F32))
    baser = e(nc.sbuf_tensor("baser", [1, TT * 8], F32))
    rtT = e(nc.sbuf_tensor("rtT", [128, 128], BF16))
    sgbc = e(nc.sbuf_tensor("sgbc", [128, 22], F32))
    subc = e(nc.sbuf_tensor("subc", [128, 22], F32))
    sdbr = e(nc.sbuf_tensor("sdbr", [1, D], BF16))
    ones_r = e(nc.sbuf_tensor("ones_r", [1, 128], BF16))
    onescol = e(nc.sbuf_tensor("onescol", [128, 1], F32))
    onesrow = e(nc.sbuf_tensor("onesrow", [1, 128], F32))
    carry = e(nc.sbuf_tensor("carry", [128, 16], F32))
    Tpart = e(nc.sbuf_tensor("Tpart", [128, 16], F32))
    Spart = e(nc.sbuf_tensor("Spart", [128, 16], F32))
    mask_sb = e(nc.sbuf_tensor("mask_sb", [128, 1], F32))
    eps_col = e(nc.sbuf_tensor("eps_col", [128, 1], F32))
    idf_sb = e(nc.sbuf_tensor("idf_sb", [128, 128], F32))
    idfb_sb = e(nc.sbuf_tensor("idfb_sb", [128, 128], BF16))
    triU_sb = e(nc.sbuf_tensor("triU_sb", [128, 128], F32))
    iotaF_sb = e(nc.sbuf_tensor("iotaF_sb", [128, 128], F32))
    iota2_sb = e(nc.sbuf_tensor("iota2_sb", [128, 2 * TT], BF16))
    eCb_sb = e(nc.sbuf_tensor("eCb_sb", [128, 8], F32))
    oh = e(nc.sbuf_tensor("oh", [128, NSB * 128], BF16))
    sqb = e(nc.sbuf_tensor("sqb", [128, D], BF16))

    ps_gu = e(nc.psum_tensor("ps_gu", [128, 2048], F32))
    ps_dn = e(nc.psum_tensor("ps_dn", [128, 1024], F32))

    sems = {nm: e(nc.semaphore(nm)) for nm in SEM_NAMES}

    cache = {}

    def phase_buffers(key, factory):
        if key not in cache:
            pst_ = ExitStack()
            cache[key] = [pst_, factory(pst_), False]
        return cache[key][1]

    def phase_free(key, first_walk):
        if first_walk and key in cache and not cache[key][2]:
            cache[key][0].close()
            cache[key][2] = True

    kCH = lambda k: k * CH

    def schedule(w: Walk, first_walk):
        SY, GP, TE, SC, VE = (w.me("sync"), w.me("gpsimd"), w.me("tensor"),
                              w.me("scalar"), w.me("vector"))
        E = w.E

        bankfree = {b: ("v", 0) for b in range(8)}

        def bank_wait(banks, who="tensor"):
            for b in banks:
                s_, val = bankfree[b]
                w.wait(who, s_, val)

        def bank_set(banks, s_, val):
            for b in banks:
                bankfree[b] = (s_, val)

        def barrier():
            vals = dict(w.n)
            for who in ENGS:
                for s_, val in vals.items():
                    w.wait(who, s_, val)

        pro = phase_buffers("pro", lambda ps: dict(
            pst=ps.enter_context(nc.psum_tensor("pstP", [128, 2048], BF16)),
            wtmp=ps.enter_context(nc.sbuf_tensor("wtmp", [128, 2 * D], F32, side="right")),
            wbf=ps.enter_context(nc.sbuf_tensor("wbf", [128, 2 * D], BF16, side="right")),
            stg=ps.enter_context(nc.sbuf_tensor("stg", [128, 2 * D], BF16, side="right")),
            rtmp=ps.enter_context(nc.sbuf_tensor("rtmp", [8, D], F32, side="right")),
            rbf=ps.enter_context(nc.sbuf_tensor("rbf", [8, D], BF16, side="right")),
            btmp=ps.enter_context(nc.sbuf_tensor("btmp", [22, 2 * 128], F32, side="right")),
            sdbt=ps.enter_context(nc.sbuf_tensor("sdbt", [1, D], F32, side="right")),
        ))
        wtmp, wbf, stg, pstP = pro["wtmp"], pro["wbf"], pro["stg"], pro["pst"]

        ins = E.dma_start(out=idf_sb[:], in_=idf) if SY else None
        w.fire(ins, "dsm", 16)
        ins = E.dma_start(out=idfb_sb[:], in_=idfb) if SY else None
        w.fire(ins, "dsm", 16)
        ins = E.dma_start(out=triU_sb[:], in_=triU) if SY else None
        w.fire(ins, "dsm", 16)
        ins = E.dma_start(out=iotaF_sb[:], in_=iotaF) if SY else None
        w.fire(ins, "dsm", 16)
        ins = E.dma_start(out=iota2_sb[:], in_=iota2) if SY else None
        w.fire(ins, "dsm", 16)
        ins = E.dma_start(out=eCb_sb[:], in_=eCbd) if SY else None
        w.fire(ins, "dsm", 16)
        ins = E.dma_start(out=mask_sb[:], in_=maskv) if SY else None
        w.fire(ins, "dsm", 16)
        ins = E.dma_start(out=pro["rtmp"][:], in_=router) if SY else None
        w.fire(ins, "dsm", 16)
        ins = E.dma_start(out=pro["btmp"][:, 0:128],
                          in_=s_gate_b.rearrange("(t p) -> t p", p=128)) if SY else None
        w.fire(ins, "dsm", 16)
        ins = E.dma_start(out=pro["btmp"][:, 128:256],
                          in_=s_up_b.rearrange("(t p) -> t p", p=128)) if SY else None
        w.fire(ins, "dsm", 16)
        ins = E.dma_start(out=pro["sdbt"][:],
                          in_=s_down_b.rearrange("(a d) -> a d", a=1)) if SY else None
        v_small = w.fire(ins, "dsm", 16)

        for tgt, val in ((ones_r, 1.0), (onescol, 1.0), (onesrow, 1.0),
                         (eps_col, EPS), (carry, 0.0)):
            ins = E.memset(tgt[:], val) if GP else None
            v_msets = w.fire(ins, "misc", 1)

        pro_cast_v = [0, 0]
        pro_tr_b = [0, 0]
        pro_ev_b = [0, 0]
        pro_st_slot = [0, 0]
        for m, src in enumerate([Wq, Wk, Wv, Wo]):
            for eb in range(_NEB):
                slot = eb % 2
                w.wait("sync", "v", pro_cast_v[slot])
                ins = E.dma_start(out=wtmp[:, slot * D:(slot + 1) * D],
                                  in_=src[eb * 128:(eb + 1) * 128, :]) if SY else None
                v_ld = w.fire(ins, "din%d" % slot, 16)
                w.wait("vector", "din%d" % slot, v_ld)
                w.wait("vector", "p", pro_tr_b[0])
                w.wait("vector", "p", pro_tr_b[1])
                ins = E.tensor_copy(out=wbf[:, slot * D:(slot + 1) * D],
                                    in_=wtmp[:, slot * D:(slot + 1) * D]) if VE else None
                v_cast = w.fire(ins, "v", 1)
                pro_cast_v[slot] = v_cast
                if m == 0 and eb == 0:
                    w.wait("tensor", "dsm", v_small)
                w.wait("tensor", "v", v_cast)
                for half in range(2):
                    bank_wait([6 + half])
                    last = None
                    for j in range(8):
                        k = half * 8 + j
                        if TE:
                            last = E.transpose(
                                pstP[:, half * 1024 + j * 128: half * 1024 + (j + 1) * 128],
                                wbf[:, slot * D + k * 128: slot * D + (k + 1) * 128],
                                idfb_sb[:])
                    p_tr = w.fire(last, "p", 1)
                    pro_tr_b[half] = p_tr
                    w.wait("vector", "p", p_tr)
                    w.wait("vector", "wst%d" % slot, pro_st_slot[slot])
                    if VE:
                        if m == 3:
                            ins = E.tensor_scalar(
                                out=stg[:, slot * D + half * 1024: slot * D + (half + 1) * 1024],
                                in0=pstP[:, half * 1024:(half + 1) * 1024],
                                scalar1=DENOM, scalar2=None, op0=ALU.mult)
                        else:
                            ins = E.tensor_copy(
                                out=stg[:, slot * D + half * 1024: slot * D + (half + 1) * 1024],
                                in_=pstP[:, half * 1024:(half + 1) * 1024])
                    else:
                        ins = None
                    v_ev = w.fire(ins, "v", 1)
                    pro_ev_b[half] = v_ev
                    bank_set([6 + half], "v", v_ev)
                w.wait("scalar", "v", pro_ev_b[0])
                w.wait("scalar", "v", pro_ev_b[1])
                dst = wtd[m][:, eb] if m < 3 else wod[:, :, eb, :]
                ins = E.dma_start(
                    out=dst,
                    in_=stg[:, slot * D:(slot + 1) * D].rearrange(
                        "p (k n) -> p k n", k=16)) if SC else None
                pro_st_slot[slot] = w.fire(ins, "wst%d" % slot, 16)

        w.wait("vector", "dsm", v_small)
        ins = E.tensor_copy(out=pro["rbf"][:], in_=pro["rtmp"][:]) if VE else None
        v_rbf = w.fire(ins, "v", 1)
        w.wait("tensor", "v", v_rbf)
        bank_wait([6])
        last = None
        for k in range(16):
            if TE:
                last = E.transpose(pstP[:, k * 8:(k + 1) * 8],
                                   pro["rbf"][:, k * 128:(k + 1) * 128],
                                   idfb_sb[0:8, 0:8])
        p_rt = w.fire(last, "p", 1)
        w.wait("vector", "p", p_rt)
        ins = E.tensor_copy(out=rtT[:, 0:128], in_=pstP[:, 0:128]) if VE else None
        v_rt = w.fire(ins, "v", 1)
        bank_set([6], "v", v_rt)

        bank_wait([0])
        if TE:
            E.transpose(ps_gu[:, 0:22], pro["btmp"][:, 0:128], idf_sb[0:22, 0:22])
            ins = E.transpose(ps_gu[:, 32:54], pro["btmp"][:, 128:256],
                              idf_sb[0:22, 0:22])
        p_b = w.fire(ins if TE else None, "p", 1)
        w.wait("vector", "p", p_b)
        if VE:
            E.tensor_copy(out=sgbc[:], in_=ps_gu[:, 0:22])
            ins = E.tensor_copy(out=subc[:], in_=ps_gu[:, 32:54])
        v_bc = w.fire(ins if VE else None, "v", 1)
        bank_set([0], "v", v_bc)
        w.wait("vector", "dsm", v_small)
        ins = E.tensor_copy(out=sdbr[:], in_=pro["sdbt"][:]) if VE else None
        w.fire(ins, "v", 1)
        w.wait("vector", "misc", v_msets)

        phase_free("pro", first_walk)
        barrier()

        def fresh_ln_state():
            return dict(xld_gate={0: [], 1: []}, xnb_gate={0: [], 1: []},
                        a_sq={}, v_xn={})

        def ln_stage(tb, ls, c, in_ap, with_router, pst):
            xld, xnb, xnT = tb["xld"], tb["xnb"], tb["xnT"]
            v_xnT_last = 0
            for t in range(TT):
                tok0 = c * CH + t * 128
                sl = t % 2
                xsl = xld[:, sl * D:(sl + 1) * D]
                for (s_, val) in ls["xld_gate"][sl]:
                    w.wait("sync", s_, val)
                ins = E.dma_start(out=xsl, in_=in_ap[tok0:tok0 + 128, :]) if SY else None
                v_xin = w.fire(ins, "xld%d" % sl, 16)

                cS = stat[:, t * 8 + 0:t * 8 + 1]
                cQ = stat[:, t * 8 + 1:t * 8 + 2]
                cM = stat[:, t * 8 + 2:t * 8 + 3]
                cT = stat[:, t * 8 + 3:t * 8 + 4]
                cNV = stat[:, t * 8 + 4:t * 8 + 5]
                cL = stat[:, t * 8 + 5:t * 8 + 6]
                cR = stat[:, t * 8 + 6:t * 8 + 7]
                xnsl = xnb[:, sl * D:(sl + 1) * D]

                w.wait("vector", "xld%d" % sl, v_xin)
                ins = E.reduce_sum(cS, xsl, axis=AX.X) if VE else None
                v_rs = w.fire(ins, "v", 1)
                w.wait("scalar", "xld%d" % sl, v_xin)
                ins = E.activation(out=sqb[:], in_=xsl, func=AF.Square,
                                   accum_out=cQ) if SC else None
                a_sq = w.fire(ins, "a", 1)
                w.wait("vector", "a", a_sq)
                w.wait("vector", "v", v_rs)
                if VE:
                    E.tensor_scalar(out=cM, in0=cS, scalar1=1.0 / D, scalar2=None,
                                    op0=ALU.mult)
                    ins = E.tensor_scalar(out=cT, in0=cQ, scalar1=1.0 / D,
                                          scalar2=None, op0=ALU.mult)
                v_m = w.fire(ins if VE else None, "v", 1)
                w.wait("vector", "v", v_m)
                ins = E.scalar_tensor_tensor(out=cNV, in0=cM, scalar=cM, in1=cT,
                                             op0=ALU.mult, op1=ALU.subtract) if VE else None
                v_nv = w.fire(ins, "v", 1)
                w.wait("scalar", "v", v_nv)
                ins = E.activation(out=cL, in_=cNV, func=AF.Ln, bias=eps_col[:],
                                   scale=-1.0) if SC else None
                a_l = w.fire(ins, "a", 1)
                w.wait("scalar", "a", a_l)
                ins = E.activation(out=cR, in_=cL, func=AF.Exp, scale=-0.5) if SC else None
                a_r = w.fire(ins, "a", 1)
                w.wait("vector", "a", a_r)
                for (s_, val) in ls["xnb_gate"][sl]:
                    w.wait("vector", s_, val)
                ins = E.tensor_scalar(out=xnsl, in0=xsl, scalar1=cM, scalar2=cR,
                                      op0=ALU.subtract, op1=ALU.mult) if VE else None
                v_xn = w.fire(ins, "v", 1)
                ls["a_sq"][t] = a_sq
                ls["v_xn"][t] = v_xn
                ls["xld_gate"][sl] = [("a", a_sq), ("v", v_xn)]

                if tb.get("store_xnd"):
                    w.wait("sync", "v", v_xn)
                    ins = E.dma_start(out=xnd[c][t * 128:(t + 1) * 128, :],
                                      in_=xnsl) if SY else None
                    v_xst = w.fire(ins, "xnb%d" % sl, 16)
                    ls.setdefault("xnst", {})[t] = v_xst
                    xnb_gate_extra = [("xnb%d" % sl, v_xst)]
                else:
                    xnb_gate_extra = []

                w.wait("tensor", "v", v_xn)
                p_tr_halves = []
                for half in range(2):
                    bank_wait([6 + half])
                    last = None
                    for j in range(8):
                        k = half * 8 + j
                        if TE:
                            last = E.transpose(
                                pst[:, half * 1024 + j * 128: half * 1024 + (j + 1) * 128],
                                xnsl[:, k * 128:(k + 1) * 128], idfb_sb[:])
                    p_tr = w.fire(last, "p", 1)
                    p_tr_halves.append(p_tr)
                    w.wait("vector", "p", p_tr)
                    ins = E.tensor_copy(
                        out=xnT[:].rearrange("p (k n) -> p k n", k=KT)[
                            :, half * 8:(half + 1) * 8, t * 128:(t + 1) * 128],
                        in_=pst[:, half * 1024:(half + 1) * 1024].rearrange(
                            "p (k n) -> p k n", k=8)) if VE else None
                    v_xnT = w.fire(ins, "v", 1)
                    bank_set([6 + half], "v", v_xnT)
                    v_xnT_last = v_xnT
                ls["xnb_gate"][sl] = xnb_gate_extra + [("p", p_tr_halves[1])]

                if with_router:
                    w.wait("tensor", "v", v_xnT_last)
                    lg_off = (t % 2) * 1024
                    bank_wait([0 + (t % 2) * 2])
                    last = None
                    for k in range(KT):
                        if TE:
                            last = E.matmul(
                                ps_gu[:, lg_off + 0:lg_off + 8],
                                xnT[:, kCH(k) + t * 128: kCH(k) + (t + 1) * 128],
                                rtT[:, k * 8:(k + 1) * 8],
                                start=(k == 0), stop=(k == KT - 1))
                    p_lg = w.fire(last, "p", 1)
                    w.wait("vector", "p", p_lg)
                    ins = E.tensor_copy(out=logit[:], in_=ps_gu[:, lg_off:lg_off + 8]) if VE else None
                    v_lg = w.fire(ins, "v", 1)
                    w.wait("vector", "v", v_lg)
                    ins = E.max(out=mx[:, t * 8:(t + 1) * 8], in_=logit[:]) if VE else None
                    v_mx = w.fire(ins, "v", 1)
                    w.wait("vector", "v", v_mx)
                    m1 = mx[:, t * 8 + 0:t * 8 + 1]
                    m2 = mx[:, t * 8 + 1:t * 8 + 2]
                    if VE:
                        E.tensor_sub(dd[:, t * 2 + 0:t * 2 + 1], m1, m2)
                        ins = E.tensor_sub(dd[:, t * 2 + 1:t * 2 + 2], m2, m1)
                    v_dd = w.fire(ins if VE else None, "v", 1)
                    w.wait("scalar", "v", v_dd)
                    ins = E.activation(out=ww[:, t * 2:t * 2 + 2],
                                       in_=dd[:, t * 2:t * 2 + 2],
                                       func=AF.Sigmoid) if SC else None
                    a_w = w.fire(ins, "a", 1)
                    w.wait("vector", "a", a_w)
                    ins = E.tensor_scalar(out=w1p[:, t * 2:t * 2 + 2],
                                          in0=ww[:, t * 2:t * 2 + 2], scalar1=0.5,
                                          scalar2=None, op0=ALU.mult) if VE else None
                    w.fire(ins, "v", 1)
                    if VE:
                        E.tensor_scalar(out=msk1[:, t * 8:(t + 1) * 8], in0=logit[:],
                                        scalar1=m1, scalar2=None, op0=ALU.is_equal)
                        ins = E.tensor_scalar(out=msk2[:, t * 8:(t + 1) * 8], in0=logit[:],
                                              scalar1=m2, scalar2=None, op0=ALU.is_equal)
                    v_mk = w.fire(ins if VE else None, "v", 1)
                    w.wait("vector", "v", v_mk)
                    ins = E.tensor_add(mskS[:, t * 8:(t + 1) * 8],
                                       msk1[:, t * 8:(t + 1) * 8],
                                       msk2[:, t * 8:(t + 1) * 8]) if VE else None
                    v_ms = w.fire(ins, "v", 1)
                    ls.setdefault("v_msk", {})[t] = v_ms
                    w.wait("tensor", "v", v_ms)
                    rk_off = (t % 2) * 1024 + 16
                    last = E.matmul(ps_gu[:, rk_off:rk_off + 8], triU_sb[:],
                                    mskS[:, t * 8:(t + 1) * 8],
                                    start=True, stop=True) if TE else None
                    p_rk = w.fire(last, "p", 1)
                    w.wait("vector", "p", p_rk)
                    ins = E.tensor_copy(out=r_e[:, t * 8:(t + 1) * 8],
                                        in_=ps_gu[:, rk_off:rk_off + 8]) if VE else None
                    v_re = w.fire(ins, "v", 1)
                    bank_set([0 + (t % 2) * 2], "v", v_re)
            return v_xnT_last

        def routing_finalize(ls):
            w.wait("tensor", "v", ls["v_msk"][TT - 1])
            bank_wait([4])
            last = E.matmul(ps_dn[0:1, 0:64], onescol[:, 0:1], mskS[:, 0:64],
                            start=True, stop=True) if TE else None
            p_tot = w.fire(last, "p", 1)
            w.wait("vector", "p", p_tot)
            ins = E.tensor_copy(out=totr[0:1, :], in_=ps_dn[0:1, 0:64]) if VE else None
            v_tot = w.fire(ins, "v", 1)
            bank_set([4], "v", v_tot)
            ins = E.memset(baser[0:1, 0:8], 0.0) if VE else None
            v_ch = w.fire(ins, "v", 1)
            for t in range(1, TT):
                w.wait("vector", "v", v_ch)
                ins = E.tensor_add(baser[0:1, t * 8:(t + 1) * 8],
                                   baser[0:1, (t - 1) * 8:t * 8],
                                   totr[0:1, (t - 1) * 8:t * 8]) if VE else None
                v_ch = w.fire(ins, "v", 1)
            w.wait("tensor", "v", v_ch)
            bank_wait([5])
            last = E.matmul(ps_dn[:, 512:576], onesrow[0:1, :], baser[0:1, 0:64],
                            start=True, stop=True) if TE else None
            p_bc = w.fire(last, "p", 1)

            gtok_first = [True]
            p_gtok_tile = {}
            v_bank5_last = 0
            for t in range(TT):
                w.wait("vector", "p", p_bc)
                ins = E.tensor_add(dest[:, t * 8:(t + 1) * 8],
                                   r_e[:, t * 8:(t + 1) * 8],
                                   ps_dn[:, 512 + t * 8:512 + (t + 1) * 8]) if VE else None
                v_d = w.fire(ins, "v", 1)
                w.wait("vector", "v", v_d)
                if VE:
                    E.tensor_add(dgt[:], dest[:, t * 8:(t + 1) * 8], eCb_sb[:])
                    ins = E.tensor_scalar(out=amask[:], in0=mskS[:, t * 8:(t + 1) * 8],
                                          scalar1=-BIG, scalar2=BIG,
                                          op0=ALU.mult, op1=ALU.add)
                v_a = w.fire(ins if VE else None, "v", 1)
                w.wait("vector", "v", v_a)
                if VE:
                    E.tensor_mul(sgt[:, 0:8], dgt[:], msk1[:, t * 8:(t + 1) * 8])
                    E.tensor_mul(sgt[:, 8:16], dgt[:], msk2[:, t * 8:(t + 1) * 8])
                    ins = E.tensor_mul(dmm[:], dest[:, t * 8:(t + 1) * 8],
                                       mskS[:, t * 8:(t + 1) * 8])
                v_s = w.fire(ins if VE else None, "v", 1)
                w.wait("vector", "v", v_s)
                if VE:
                    E.reduce_sum(slotsel[:, t * 2 + 0:t * 2 + 1], sgt[:, 0:8], axis=AX.X)
                    E.reduce_sum(slotsel[:, t * 2 + 1:t * 2 + 2], sgt[:, 8:16], axis=AX.X)
                    ins = E.tensor_add(dmm[:], dmm[:], amask[:])
                v_dm = w.fire(ins if VE else None, "v", 1)
                w.wait("vector", "v", v_dm)
                ins = None
                for s_ in range(3):
                    if VE:
                        ins = E.tensor_scalar(
                            out=dsh.rearrange("p (e s) -> p e s", s=3)[:, :, s_],
                            in0=dmm[:], scalar1=float(128 * s_), scalar2=None,
                            op0=ALU.subtract)
                v_dsh = w.fire(ins if VE else None, "v", 1)
                w.wait("vector", "v", v_dsh)
                if t > 0:
                    w.wait("vector", "p", p_gtok_tile[t - 1])
                ins = None
                for b in range(NSB):
                    if VE:
                        ins = E.tensor_scalar(out=oh[:, b * 128:(b + 1) * 128],
                                              in0=iotaF_sb[:],
                                              scalar1=dsh[:, b:b + 1],
                                              scalar2=None, op0=ALU.is_equal)
                v_oh = w.fire(ins if VE else None, "v", 1)
                w.wait("tensor", "v", v_oh)
                if gtok_first[0]:
                    bank_wait([1])
                last = None
                for b in range(NSB):
                    if TE:
                        last = E.matmul(ps_gu[:, 512 + 2 * b:512 + 2 * b + 2],
                                        oh[:, b * 128:(b + 1) * 128],
                                        iota2_sb[:, 2 * t:2 * t + 2],
                                        start=gtok_first[0],
                                        stop=(t == TT - 1 and b == NSB - 1),
                                        skip_group_check=True)
                    gtok_first[0] = False
                p_gt = w.fire(last, "p", 1)
                p_gtok_tile[t] = p_gt
            bank_set([5], "v", w.n["v"])

            w.wait("vector", "p", p_gtok_tile[TT - 1])
            ins = E.tensor_copy(out=gtmp[:], in_=ps_gu[:, 512:512 + 2 * NSB]) if VE else None
            v_g1 = w.fire(ins, "v", 1)
            bank_set([1], "v", v_g1)
            w.wait("vector", "v", v_g1)
            ins = E.tensor_add(gtokf[:],
                               gtmp[:].rearrange("p (b two) -> p b two", two=2)[:, :, 0],
                               gtmp[:].rearrange("p (b two) -> p b two", two=2)[:, :, 1]) if VE else None
            v_g2 = w.fire(ins, "v", 1)
            w.wait("vector", "v", v_g2)
            if VE:
                E.tensor_copy(out=idxu[:], in_=gtokf[:])
                ins = E.tensor_copy(out=slotu[:], in_=slotsel[:])
            v_idx = w.fire(ins if VE else None, "v", 1)
            return v_idx

        def ffn_phase(key, in_ap, out_ap):
            tb = phase_buffers(key, lambda ps: dict(
                pst=ps.enter_context(nc.psum_tensor(key + "pst", [128, 2048], BF16)),
                xld=ps.enter_context(nc.sbuf_tensor(key + "xld", [128, 2 * D], F32, side="right")),
                xnb=ps.enter_context(nc.sbuf_tensor(key + "xnb", [128, 2 * D], BF16, side="right")),
                xnT=ps.enter_context(nc.sbuf_tensor(key + "xnT", [128, KT * CH], BF16, side="right")),
                xg=ps.enter_context(nc.sbuf_tensor(key + "xg", [128, 4 * D], BF16, side="right")),
                xgT=ps.enter_context(nc.sbuf_tensor(key + "xgT", [128, 2 * KT * C], BF16, side="right")),
                gT=ps.enter_context(nc.sbuf_tensor(key + "gT", [128, SITL * 512], BF16, side="right")),
                dwq=ps.enter_context(nc.sbuf_tensor(key + "dwq", [128, 3 * ITL * 512], BF16, side="right")),
                gwu=ps.enter_context(nc.sbuf_tensor(key + "gwu", [128, 3 * D], BF16, side="right")),
                silu=ps.enter_context(nc.sbuf_tensor(key + "silu", [128, 2 * 512], F32, side="right")),
                y_ev=ps.enter_context(nc.sbuf_tensor(key + "yev", [128, 2 * 512], BF16, side="right")),
                ycb=ps.enter_context(nc.sbuf_tensor(key + "ycb", [128, 4 * D], BF16, side="right")),
            ))
            tb["store_xnd"] = True
            pst = tb["pst"]
            xgT, gT, dwq, gwu, silu_b, y_ev, ycb = (tb["xgT"], tb["gT"], tb["dwq"],
                                                    tb["gwu"], tb["silu"], tb["y_ev"],
                                                    tb["ycb"])
            ls = fresh_ln_state()
            gwu_slot_p = [0, 0, 0]
            dwq_slot_p = [0, 0, 0]
            xg_slot_p = [0, 0, 0, 0]
            yev_v = [0, 0]
            yev_st = [0, 0]
            gm_slot = [0, 0]
            xgT_half_p = [0, 0]
            ycb_gate = [[] for _ in range(4)]
            xos_gate = {0: [], 1: []}

            units = [("s", 0), ("s", 1)] + [("e", ee) for ee in range(8)]

            gwc_st = [0, 0, 0]
            dwc_st = [0, 0, 0]
            cache_ready = []
            for c in range(_NCH):
                use_cache = key == "f3" or c > 0
                fill_cache = key == "f1" and c == 0
                v_xnT_all = ln_stage(tb, ls, c, in_ap, True, pst)
                v_idx = routing_finalize(ls)

                xg_ld = {}

                def emit_gather(j):
                    slot = j % 4
                    w.wait("gpsimd", "v", v_idx)
                    w.wait("gpsimd", "xnb0", ls["xnst"].get(6, 0))
                    w.wait("gpsimd", "xnb1", ls["xnst"].get(7, 0))
                    w.wait("gpsimd", "p", xg_slot_p[slot])
                    ins = E.indirect_dma_start(
                        out=tb["xg"][:, slot * D:(slot + 1) * D],
                        out_offset=None,
                        in_=xnd[c],
                        in_offset=IOffs(ap=idxu[:, j:j + 1], axis=0)) if GP else None
                    xg_ld[j] = w.fire(ins, "xg%d" % slot, 16)

                xgT_ready = {}

                def emit_transposes(ee):
                    h_ = ee % 2
                    v_last = 0
                    for s_ in range(3):
                        j = ee * 3 + s_
                        slot = j % 4
                        w.wait("tensor", "xg%d" % slot, xg_ld[j])
                        for half in range(2):
                            bank_wait([6 + half])
                            last = None
                            for jj in range(8):
                                k = half * 8 + jj
                                if TE:
                                    last = E.transpose(
                                        pst[:, half * 1024 + jj * 128:
                                            half * 1024 + (jj + 1) * 128],
                                        tb["xg"][:, slot * D + k * 128:
                                                 slot * D + (k + 1) * 128],
                                        idfb_sb[:])
                            p_tr = w.fire(last, "p", 1)
                            xg_slot_p[slot] = p_tr
                            w.wait("vector", "p", p_tr)
                            w.wait("vector", "p", xgT_half_p[h_])
                            ins = E.tensor_copy(
                                out=xgT[:, h_ * KT * C:(h_ + 1) * KT * C].rearrange(
                                    "p (k n) -> p k n", k=KT)[
                                    :, half * 8:(half + 1) * 8,
                                    s_ * 128:(s_ + 1) * 128],
                                in_=pst[:, half * 1024:(half + 1) * 1024].rearrange(
                                    "p (k n) -> p k n", k=8)) if VE else None
                            v_last = w.fire(ins, "v", 1)
                            bank_set([6 + half], "v", v_last)
                    xgT_ready[ee] = v_last

                p_gu_last = 0
                for ui, (kind, uu) in enumerate(units):
                    routed = kind == "e"
                    n_tok = C if routed else 512
                    n_it = ITL if routed else SITL
                    n_st = 3 if routed else 4
                    if routed:
                        gsrc = lambda it, ee=uu: gate_w[ee][:, it * 128:(it + 1) * 128]
                        usrc = lambda it, ee=uu: up_w[ee][:, it * 128:(it + 1) * 128]
                        dsrc_q = lambda q, ee=uu: down_w[ee][:, q * 512:(q + 1) * 512]
                        rowbase = uu * C
                    else:
                        gsrc = lambda it: s_gate_w[:, it * 128:(it + 1) * 128]
                        usrc = lambda it: s_up_w[:, it * 128:(it + 1) * 128]
                        dsrc_q = lambda q: s_down_w[:, q * 512:(q + 1) * 512]
                        rowbase = 8 * C + uu * 512

                    if ui == 1:
                        for j in range(3):
                            emit_gather(j)
                    if routed:
                        v_act_ready = xgT_ready[uu]
                        hh_ = uu % 2
                        mv = lambda k, xgT=xgT, hh_=hh_: xgT[
                            :, hh_ * KT * C + k * C: hh_ * KT * C + k * C + C]
                    else:
                        v_act_ready = v_xnT_all
                        off = uu * 512
                        mv = lambda k, off=off: tb["xnT"][:, kCH(k) + off: kCH(k) + off + 512]

                    wgu_v = {}

                    def load_gu(jj, it, which):
                        slot = jj % 3
                        if not use_cache:
                            w.wait("gpsimd", "p", gwu_slot_p[slot])
                            w.wait("gpsimd", "wgc%d" % slot, gwc_st[slot])
                            src = gsrc(it) if which == 0 else usrc(it)
                            ins = E.dma_start(
                                out=gwu[:, slot * D:(slot + 1) * D].rearrange(
                                    "p (k n) -> p k n", k=KT),
                                in_=src.rearrange("(k p) i -> p k i", p=128)) if GP else None
                            v_l = w.fire(ins, "wgu%d" % slot, 16)
                            if fill_cache:
                                if routed:
                                    cdst = (gwc if which == 0 else uwc)[uu, it]
                                else:
                                    cdst = (sgwc if which == 0 else suwc)[it]
                                w.wait("scalar", "wgu%d" % slot, v_l)
                                ins = E.dma_start(out=cdst,
                                                  in_=gwu[:, slot * D:(slot + 1) * D]) if SC else None
                                gwc_st[slot] = w.fire(ins, "wgc%d" % slot, 16)
                        else:
                            eng = ("sync", "gpsimd", "scalar")[slot]
                            ON = (SY, GP, SC)[slot]
                            w.wait(eng, "p", gwu_slot_p[slot])
                            for qq, vv in cache_ready:
                                w.wait(eng, qq, vv)
                            if routed:
                                csrc = (gwc if which == 0 else uwc)[uu, it]
                            else:
                                csrc = (sgwc if which == 0 else suwc)[it]
                            ins = E.dma_start(out=gwu[:, slot * D:(slot + 1) * D],
                                              in_=csrc) if ON else None
                            v_l = w.fire(ins, "wgu%d" % slot, 16)
                        wgu_v[jj] = (v_l, slot)

                    load_gu(0, 0, 0)
                    load_gu(1, 0, 1)
                    load_gu(2, 1, 0)

                    dwq_v = {}

                    def load_dw(jq, q, part):
                        slot = jq % 3
                        if not use_cache:
                            w.wait("gpsimd", "p", dwq_slot_p[slot])
                            w.wait("gpsimd", "wdc%d" % slot, dwc_st[slot])
                            if routed:
                                srcap = dsrc_q(q).rearrange("(i p) n -> p i n", p=128)
                            else:
                                srcap = dsrc_q(q)[part * 1408:(part + 1) * 1408, :].rearrange(
                                    "(i p) n -> p i n", p=128)
                            ins = E.dma_start(
                                out=dwq[:, slot * ITL * 512:(slot + 1) * ITL * 512].rearrange(
                                    "p (i n) -> p i n", i=ITL),
                                in_=srcap) if GP else None
                            v_l = w.fire(ins, "wdw%d" % slot, 16)
                            if fill_cache:
                                cdst = dwc[uu, q] if routed else sdwc[q, part]
                                w.wait("scalar", "wdw%d" % slot, v_l)
                                ins = E.dma_start(
                                    out=cdst,
                                    in_=dwq[:, slot * ITL * 512:(slot + 1) * ITL * 512]) if SC else None
                                dwc_st[slot] = w.fire(ins, "wdc%d" % slot, 16)
                        else:
                            eng = ("scalar", "sync", "gpsimd")[slot]
                            ON = (SC, SY, GP)[slot]
                            w.wait(eng, "p", dwq_slot_p[slot])
                            for qq, vv in cache_ready:
                                w.wait(eng, qq, vv)
                            csrc = dwc[uu, q] if routed else sdwc[q, part]
                            ins = E.dma_start(
                                out=dwq[:, slot * ITL * 512:(slot + 1) * ITL * 512],
                                in_=csrc) if ON else None
                            v_l = w.fire(ins, "wdw%d" % slot, 16)
                        dwq_v[jq] = (v_l, slot)

                    nparts = 1 if routed else 2
                    for jq in range(3):
                        load_dw(jq, jq // nparts, jq % nparts)

                    v_g_last = 0
                    jj = 2
                    for it in range(n_it):
                        g_v, g_slot = wgu_v[2 * it]
                        u_v, u_slot = wgu_v[2 * it + 1]
                        pslot = it % 2
                        g_ps = ps_gu[:, pslot * 1024 + 0: pslot * 1024 + n_tok]
                        u_ps = ps_gu[:, pslot * 1024 + 512: pslot * 1024 + 512 + n_tok]
                        gbank = [0 + pslot * 2]
                        ubank = [1 + pslot * 2]

                        w.wait("tensor", "wgu%d" % g_slot, g_v)
                        if ui == 0 and it == 0:
                            w.wait("tensor", "v", v_act_ready)
                        if routed and it == 0:
                            w.wait("tensor", "v", v_act_ready)
                        bank_wait(gbank)
                        last = None
                        for k in range(KT):
                            if TE:
                                last = E.matmul(
                                    g_ps,
                                    gwu[:, g_slot * D + k * 128: g_slot * D + (k + 1) * 128],
                                    mv(k), start=(k == 0), stop=(k == KT - 1))
                        p_g = w.fire(last, "p", 1)
                        gwu_slot_p[g_slot] = p_g
                        w.wait("tensor", "wgu%d" % u_slot, u_v)
                        bank_wait(ubank)
                        last = None
                        for k in range(KT):
                            if TE:
                                last = E.matmul(
                                    u_ps,
                                    gwu[:, u_slot * D + k * 128: u_slot * D + (k + 1) * 128],
                                    mv(k), start=(k == 0), stop=(k == KT - 1))
                        p_u = w.fire(last, "p", 1)
                        gwu_slot_p[u_slot] = p_u
                        p_gu_last = p_u
                        if routed:
                            xgT_half_p[uu % 2] = p_u
                        jj += 1
                        if jj < 2 * n_it:
                            load_gu(jj, jj // 2, jj % 2)
                            jj += 1
                        if jj < 2 * n_it:
                            load_gu(jj, jj // 2, jj % 2)

                        w.wait("scalar", "p", p_g)
                        w.wait("scalar", "v", gm_slot[pslot])
                        sl = silu_b[:, pslot * 512: pslot * 512 + n_tok]
                        if SC:
                            if routed:
                                ins = E.activation(out=sl, in_=g_ps, func=AF.Silu)
                            else:
                                ins = E.activation(out=sl, in_=g_ps, func=AF.Silu,
                                                   bias=sgbc[:, it:it + 1])
                        else:
                            ins = None
                        a_v = w.fire(ins, "a", 1)
                        bank_set(gbank, "a", a_v)

                        w.wait("vector", "a", a_v)
                        w.wait("vector", "p", p_u)
                        gsl = gT[:, it * 512: it * 512 + n_tok]
                        if VE:
                            if routed:
                                ins = E.tensor_mul(gsl, sl, u_ps)
                            else:
                                ins = E.scalar_tensor_tensor(
                                    out=gsl, in0=u_ps, scalar=subc[:, it:it + 1],
                                    in1=sl, op0=ALU.add, op1=ALU.mult)
                        else:
                            ins = None
                        v_g = w.fire(ins, "v", 1)
                        bank_set(ubank, "v", v_g)
                        gm_slot[pslot] = v_g
                        v_g_last = v_g

                    nxt = 0 if (kind == "s" and uu == 1) else (uu + 1 if routed else None)
                    if nxt is not None and nxt < 8:
                        emit_transposes(nxt)
                        if nxt + 1 < 8:
                            for j in range(3 * (nxt + 1), 3 * (nxt + 2)):
                                emit_gather(j)

                    didx = 0
                    for q in range(4):
                        base_j = q * nparts
                        needs = [(base_j + pp) for pp in range(nparts)]
                        for nj in needs:
                            if nj not in dwq_v:
                                load_dw(nj, nj // nparts, nj % nparts)
                        pf = (q + 1) * nparts
                        if pf < 4 * nparts and pf not in dwq_v:
                            load_dw(pf, pf // nparts, pf % nparts)
                        for s_ in range(n_st):
                            dn_slot = didx % 2
                            dn_ps = ps_dn[:, dn_slot * 512:(dn_slot + 1) * 512]
                            dbank = [4 + dn_slot]
                            for nj in needs:
                                w.wait("tensor", "wdw%d" % dwq_v[nj][1], dwq_v[nj][0])
                            if didx == 0:
                                w.wait("tensor", "v", v_g_last)
                            bank_wait(dbank)
                            last = None
                            for pp in range(nparts):
                                dslot = dwq_v[base_j + pp][1]
                                for it in range(ITL):
                                    git = pp * ITL + it
                                    if TE:
                                        last = E.matmul(
                                            dn_ps,
                                            gT[:, git * 512 + s_ * 128: git * 512 + (s_ + 1) * 128],
                                            dwq[:, dslot * ITL * 512 + it * 512:
                                                dslot * ITL * 512 + (it + 1) * 512],
                                            start=(pp == 0 and it == 0),
                                            stop=(pp == nparts - 1 and it == ITL - 1
                                                  and routed))
                            if not routed:
                                if TE:
                                    last = E.matmul(
                                        dn_ps, ones_r[0:1, :],
                                        sdbr[0:1, q * 512:(q + 1) * 512],
                                        start=False, stop=True)
                            p_d = w.fire(last, "p", 1)
                            if s_ == n_st - 1:
                                for nj in needs:
                                    dwq_slot_p[dwq_v[nj][1]] = p_d

                            w.wait("vector", "p", p_d)
                            ysl_ = didx % 2
                            w.wait("vector", "yev%d" % ysl_, yev_st[ysl_])
                            ins = E.tensor_copy(
                                out=y_ev[:, ysl_ * 512:(ysl_ + 1) * 512],
                                in_=dn_ps) if VE else None
                            v_ye = w.fire(ins, "v", 1)
                            yev_v[ysl_] = v_ye
                            bank_set(dbank, "v", v_ye)
                            w.wait("scalar", "v", v_ye)
                            ins = E.dma_start(
                                out=yd[c][rowbase + s_ * 128: rowbase + (s_ + 1) * 128,
                                          q * 512:(q + 1) * 512],
                                in_=y_ev[:, ysl_ * 512:(ysl_ + 1) * 512]) if SC else None
                            yev_st[ysl_] = w.fire(ins, "yev%d" % ysl_, 16)
                            didx += 1
                    dwq_v.clear()

                yst_all = (w.n["yev0"], w.n["yev1"])

                for t in range(TT):
                    sl = t % 2
                    xsl = tb["xld"][:, sl * D:(sl + 1) * D]
                    for (s_, val) in ls["xld_gate"][sl]:
                        w.wait("sync", s_, val)
                    for (s_, val) in xos_gate[sl]:
                        w.wait("sync", s_, val)
                    ins = E.dma_start(out=xsl, in_=in_ap[c * CH + t * 128:
                                                         c * CH + (t + 1) * 128, :]) if SY else None
                    v_xo = w.fire(ins, "xld%d" % sl, 16)
                    ysl, y1s, y2s = (3 * t) % 4, (3 * t + 1) % 4, (3 * t + 2) % 4
                    for (s_, val) in ycb_gate[ysl]:
                        w.wait("sync", s_, val)
                    for i in range(2):
                        w.wait("sync", "yev%d" % i, yst_all[i])
                    ins = E.dma_start(out=ycb[:, ysl * D:(ysl + 1) * D],
                                      in_=yd[c][8 * C + t * 128: 8 * C + (t + 1) * 128, :]) if SY else None
                    v_ysh = w.fire(ins, "xin%d" % ysl, 16)
                    for (s_, val) in ycb_gate[y1s]:
                        w.wait("gpsimd", s_, val)
                    for i in range(2):
                        w.wait("gpsimd", "yev%d" % i, yst_all[i])
                    w.wait("gpsimd", "v", v_idx)
                    ins = E.indirect_dma_start(
                        out=ycb[:, y1s * D:(y1s + 1) * D], out_offset=None,
                        in_=yd[c],
                        in_offset=IOffs(ap=slotu[:, 2 * t:2 * t + 1], axis=0)) if GP else None
                    v_y1 = w.fire(ins, "yc%d" % y1s, 16)
                    for (s_, val) in ycb_gate[y2s]:
                        w.wait("gpsimd", s_, val)
                    ins = E.indirect_dma_start(
                        out=ycb[:, y2s * D:(y2s + 1) * D], out_offset=None,
                        in_=yd[c],
                        in_offset=IOffs(ap=slotu[:, 2 * t + 1:2 * t + 2], axis=0)) if GP else None
                    v_y2 = w.fire(ins, "yc%d" % y2s, 16)

                    w.wait("vector", "xld%d" % sl, v_xo)
                    w.wait("vector", "xin%d" % ysl, v_ysh)
                    ins = E.scalar_tensor_tensor(out=xsl, in0=ycb[:, ysl * D:(ysl + 1) * D],
                                                 scalar=0.5, in1=xsl,
                                                 op0=ALU.mult, op1=ALU.add) if VE else None
                    v_c1 = w.fire(ins, "v", 1)
                    w.wait("vector", "v", v_c1)
                    w.wait("vector", "yc%d" % y1s, v_y1)
                    ins = E.scalar_tensor_tensor(out=xsl, in0=ycb[:, y1s * D:(y1s + 1) * D],
                                                 scalar=w1p[:, 2 * t:2 * t + 1], in1=xsl,
                                                 op0=ALU.mult, op1=ALU.add) if VE else None
                    v_c2 = w.fire(ins, "v", 1)
                    w.wait("vector", "v", v_c2)
                    w.wait("vector", "yc%d" % y2s, v_y2)
                    ins = E.scalar_tensor_tensor(out=xsl, in0=ycb[:, y2s * D:(y2s + 1) * D],
                                                 scalar=w1p[:, 2 * t + 1:2 * t + 2], in1=xsl,
                                                 op0=ALU.mult, op1=ALU.add) if VE else None
                    v_c3 = w.fire(ins, "v", 1)
                    ycb_gate[ysl] = [("v", v_c1)]
                    ycb_gate[y1s] = [("v", v_c2)]
                    ycb_gate[y2s] = [("v", v_c3)]
                    w.wait("scalar", "v", v_c3)
                    ins = E.dma_start(out=out_ap[c * CH + t * 128: c * CH + (t + 1) * 128, :],
                                      in_=xsl) if SC else None
                    v_st = w.fire(ins, "xos%d" % sl, 16)
                    xos_gate[sl] = [("xos%d" % sl, v_st)]
                    ls["xld_gate"][sl] = [("xos%d" % sl, v_st)]
                if fill_cache:
                    cache_ready = ([("wgc%d" % i, w.n["wgc%d" % i]) for i in range(3)]
                                   + [("wdc%d" % i, w.n["wdc%d" % i]) for i in range(3)])

            phase_free(key, first_walk)

        def attn_a():
            tb = phase_buffers("attnA", lambda ps: dict(
                pst=ps.enter_context(nc.psum_tensor("Apst", [128, 2048], BF16)),
                xld=ps.enter_context(nc.sbuf_tensor("Axld", [128, 2 * D], F32, side="right")),
                xnb=ps.enter_context(nc.sbuf_tensor("Axnb", [128, 2 * D], BF16, side="right")),
                xnT=ps.enter_context(nc.sbuf_tensor("AxnT", [128, KT * CH], BF16, side="right")),
                qT=ps.enter_context(nc.sbuf_tensor("AqT", [128, KT * CH], BF16, side="right")),
                wqkv=ps.enter_context(nc.sbuf_tensor("Awqkv", [128, 6 * D], BF16, side="right")),
                katt=ps.enter_context(nc.sbuf_tensor("Akatt", [128, CH], F32, side="right")),
                kv=ps.enter_context(nc.sbuf_tensor("Akv", [128, CH], F32, side="right")),
                ones_s=ps.enter_context(nc.sbuf_tensor("ones_s", [128, CH], F32, side="right")),
                kvt=ps.enter_context(nc.sbuf_tensor("Akvt", [128, 2 * CH], F32, side="right")),
            ))
            tb["store_xnd"] = False
            pst = tb["pst"]
            ls = fresh_ln_state()
            qT, wqkv = tb["qT"], tb["wqkv"]
            ones_s = tb["ones_s"]
            ins = E.memset(ones_s[:], 1.0) if GP else None
            v_ones = w.fire(ins, "misc", 1)
            w.wait("vector", "misc", v_ones)
            kvst_n = [0, 0]
            qst_prev = 0
            v_car_last = 0
            wqkv_slot_p = {m: [0, 0] for m in range(3)}

            for c in range(_NCH):
                v_xnT_all = ln_stage(tb, ls, c, x1d, False, pst)
                for g in range(16):
                    slot = g % 2
                    wv_v = {}
                    for m in range(3):
                        w.wait("gpsimd", "p", wqkv_slot_p[m][slot])
                        ins = E.dma_start(
                            out=wqkv[:, (m * 2 + slot) * D:(m * 2 + slot + 1) * D
                                     ].rearrange("p (k n) -> p k n", k=16),
                            in_=wtd[m][:, g]) if GP else None
                        wv_v[m] = w.fire(ins, "w%s%d" % ("qkv"[m], slot), 16)

                    regions = [ps_gu[:, 0:1024], ps_gu[:, 1024:2048], ps_dn[:, 0:1024]]
                    rbanks = [[0, 1], [2, 3], [4, 5]]
                    p_qkv = {}
                    for m in range(3):
                        w.wait("tensor", "w%s%d" % ("qkv"[m], slot), wv_v[m])
                        if g == 0 and m == 0:
                            w.wait("tensor", "v", v_xnT_all)
                        bank_wait(rbanks[m])
                        last = None
                        for k in range(KT):
                            for nh in range(2):
                                if TE:
                                    last = E.matmul(
                                        regions[m][:, nh * 512:(nh + 1) * 512],
                                        wqkv[:, (m * 2 + slot) * D + k * 128:
                                             (m * 2 + slot) * D + (k + 1) * 128],
                                        tb["xnT"][:, kCH(k) + nh * 512:
                                                  kCH(k) + (nh + 1) * 512],
                                        start=(k == 0), stop=(k == KT - 1))
                        p_qkv[m] = w.fire(last, "p", 1)
                        wqkv_slot_p[m][slot] = p_qkv[m]

                    w.wait("vector", "p", p_qkv[0])
                    w.wait("vector", "qst", qst_prev)
                    ins = E.tensor_copy(out=qT[:, g * CH:(g + 1) * CH],
                                        in_=ps_gu[:, 0:1024]) if VE else None
                    v_q = w.fire(ins, "v", 1)
                    bank_set([0, 1], "v", v_q)
                    w.wait("vector", "p", p_qkv[1])
                    ins = E.tensor_copy(out=tb["katt"][:], in_=ps_gu[:, 1024:2048]) if VE else None
                    v_k = w.fire(ins, "v", 1)
                    bank_set([2, 3], "v", v_k)
                    w.wait("vector", "p", p_qkv[2])
                    w.wait("vector", "v", v_k)
                    ins = E.tensor_mul(tb["kv"][:], tb["katt"][:],
                                       ps_dn[:, 0:1024]) if VE else None
                    v_kv = w.fire(ins, "v", 1)
                    bank_set([4, 5], "v", v_kv)
                    w.wait("vector", "kvst%d" % slot, kvst_n[slot])
                    w.wait("vector", "v", v_kv)
                    ins = E.tensor_tensor_scan(
                        out=tb["kvt"][:, slot * CH:(slot + 1) * CH],
                        data0=ones_s[:], data1=tb["kv"][:],
                        initial=carry[:, g:g + 1],
                        op0=ALU.mult, op1=ALU.add) if VE else None
                    v_sc = w.fire(ins, "v", 1)
                    w.wait("vector", "v", v_sc)
                    ins = E.tensor_copy(
                        out=carry[:, g:g + 1],
                        in_=tb["kvt"][:, slot * CH + CH - 1: slot * CH + CH]) if VE else None
                    v_cr = w.fire(ins, "v", 1)
                    w.wait("vector", "v", v_cr)
                    v_car_last = v_cr
                    w.wait("scalar", "v", v_sc)
                    ins = E.dma_start(out=kvtd[g][:, c * CH:(c + 1) * CH],
                                      in_=tb["kvt"][:, slot * CH:(slot + 1) * CH]) if SC else None
                    kvst_n[slot] = w.fire(ins, "kvst%d" % slot, 16)

                w.wait("scalar", "v", w.n["v"])
                ins = E.dma_start(out=qTd[c], in_=qT[:]) if SC else None
                qst_prev = w.fire(ins, "qst", 16)

            w.wait("scalar", "v", v_car_last)
            ins = E.dma_start(out=cc_in, in_=carry[:]) if SC else None
            v_ci = w.fire(ins, "ccin", 16)
            w.wait("gpsimd", "ccin", v_ci)
            ins = E.collective_compute(
                "AllGather", ALU.bypass,
                replica_groups=[[0, 1], [2, 3], [4, 5], [6, 7]],
                ins=[cc_in.opt()], outs=[cc_out.opt()]) if GP else None
            v_cc = w.fire(ins, "cc", 1)
            w.wait("gpsimd", "cc", v_cc)
            ins = E.dma_start(out=Tpart[:], in_=cc_out[0:128, :]) if GP else None
            v_co = w.fire(ins, "ccout", 16)
            w.wait("vector", "ccout", v_co)
            ins = E.tensor_scalar(out=Spart[:], in0=Tpart[:], scalar1=mask_sb[:, 0:1],
                                  scalar2=None, op0=ALU.mult) if VE else None
            v_S = w.fire(ins, "v", 1)
            w.wait("vector", "v", v_S)
            phase_free("attnA", first_walk)
            return (w.n["kvst0"], w.n["kvst1"], w.n["qst"])

        def attn_b(kvst_totals):
            tb = phase_buffers("attnB", lambda ps: dict(
                ps_b2=ps.enter_context(nc.psum_tensor("Bps2", [128, 1024], F32)),
                qT=ps.enter_context(nc.sbuf_tensor("BqT", [128, 2 * KT * CH], BF16, side="right")),
                kvt=ps.enter_context(nc.sbuf_tensor("Bkvt", [128, KT * CH], F32, side="right")),
                wo=ps.enter_context(nc.sbuf_tensor("Bwo", [128, 2 * D], BF16, side="right")),
                xb=ps.enter_context(nc.sbuf_tensor("Bxb", [128, 4 * D], F32, side="right")),
            ))
            ps_b2 = tb["ps_b2"]
            qTb, kvtC, wob, xb = tb["qT"], tb["kvt"], tb["wo"], tb["xb"]
            v_vh_prev = 0
            wo_slot_p = [0, 0]
            qT_slot_p = [0, 0]
            xb_gate = {i: [] for i in range(4)}
            regionsA = [ps_gu[:, j * 512:(j + 1) * 512] for j in range(4)]
            banksA = [[0], [1], [2], [3]]
            regionsB = [ps_dn[:, 0:512], ps_dn[:, 512:1024],
                        ps_b2[:, 0:512], ps_b2[:, 512:1024]]
            banksB = [[4], [5], [6], [7]]
            for c in range(_NCH):
                if c == 0:
                    w.wait("sync", "kvst0", kvst_totals[0])
                    w.wait("sync", "kvst1", kvst_totals[1])
                w.wait("sync", "v", v_vh_prev)
                w.wait("gpsimd", "v", v_vh_prev)
                v_kvld = 0
                v_kvld2 = 0
                for g in range(16):
                    if g % 2 == 0:
                        ins = E.dma_start(out=kvtC[:, g * CH:(g + 1) * CH],
                                          in_=kvtd[g][:, c * CH:(c + 1) * CH]) if SY else None
                        v_kvld = w.fire(ins, "kvld", 16)
                    else:
                        ins = E.dma_start(out=kvtC[:, g * CH:(g + 1) * CH],
                                          in_=kvtd[g][:, c * CH:(c + 1) * CH]) if GP else None
                        v_kvld2 = w.fire(ins, "kvld2", 16)
                if c == 0:
                    w.wait("sync", "qst", kvst_totals[2])
                qsl = c % 2
                qS = qTb[:, qsl * KT * CH:(qsl + 1) * KT * CH]
                w.wait("sync", "p", qT_slot_p[qsl])
                ins = E.dma_start(out=qS, in_=qTd[c]) if SY else None
                v_qld = w.fire(ins, "qld", 16)

                v_vh = 0
                for g in range(16):
                    w.wait("vector", "kvld", v_kvld)
                    w.wait("vector", "kvld2", v_kvld2)
                    w.wait("vector", "qld", v_qld)
                    w.wait("vector", "p", qT_slot_p[qsl])
                    ins = E.scalar_tensor_tensor(
                        out=qS[:, g * CH:(g + 1) * CH],
                        in0=kvtC[:, g * CH:(g + 1) * CH],
                        scalar=Spart[:, g:g + 1],
                        in1=qS[:, g * CH:(g + 1) * CH],
                        op0=ALU.add, op1=ALU.mult) if VE else None
                    v_vh = w.fire(ins, "v", 1)
                v_vh_prev = v_vh

                for tp in range(TT // 2):
                    p_last = 0
                    v_xin = {}
                    for ti in range(2):
                        t = tp * 2 + ti
                        xslot = t % 4
                        for (s_, val) in xb_gate[xslot]:
                            w.wait("sync", s_, val)
                        ins = E.dma_start(out=xb[:, xslot * D:(xslot + 1) * D],
                                          in_=x1d[c * CH + t * 128: c * CH + (t + 1) * 128, :]) if SY else None
                        v_xin[ti] = w.fire(ins, "xin%d" % xslot, 16)
                    for g in range(16):
                        slot = g % 2
                        w.wait("gpsimd", "p", wo_slot_p[slot])
                        ins = E.dma_start(
                            out=wob[:, slot * D:(slot + 1) * D].rearrange(
                                "p (k n) -> p k n", k=16),
                            in_=wod[:, g]) if GP else None
                        v_wo = w.fire(ins, "wo%d" % slot, 16)
                        w.wait("tensor", "wo%d" % slot, v_wo)
                        if g == 0:
                            w.wait("tensor", "v", v_vh)
                            bank_wait(range(0, 8))
                        last = None
                        for ti in range(2):
                            t = tp * 2 + ti
                            regions = regionsA if ti == 0 else regionsB
                            for nh in range(4):
                                if TE:
                                    last = E.matmul(
                                        regions[nh],
                                        qS[:, g * CH + t * 128: g * CH + (t + 1) * 128],
                                        wob[:, slot * D + nh * 512: slot * D + (nh + 1) * 512],
                                        start=(g == 0), stop=(g == 15))
                        p_last = w.fire(last, "p", 1)
                        wo_slot_p[slot] = p_last
                    for ti in range(2):
                        t = tp * 2 + ti
                        xslot = t % 4
                        regions = regionsA if ti == 0 else regionsB
                        banks = banksA if ti == 0 else banksB
                        w.wait("vector", "p", p_last)
                        w.wait("vector", "xin%d" % xslot, v_xin[ti])
                        v_e = 0
                        for nh in range(4):
                            ins = E.tensor_add(
                                xb[:, xslot * D + nh * 512: xslot * D + (nh + 1) * 512],
                                xb[:, xslot * D + nh * 512: xslot * D + (nh + 1) * 512],
                                regions[nh]) if VE else None
                            v_e = w.fire(ins, "v", 1)
                            bank_set(banks[nh], "v", v_e)
                        w.wait("scalar", "v", v_e)
                        ins = E.dma_start(out=x2d[c * CH + t * 128: c * CH + (t + 1) * 128, :],
                                          in_=xb[:, xslot * D:(xslot + 1) * D]) if SC else None
                        st_v = w.fire(ins, "xst%d" % xslot, 16)
                        xb_gate[xslot] = [("xst%d" % xslot, st_v)]
                qT_slot_p[qsl] = p_last
            phase_free("attnB", first_walk)

        ffn_phase("f1", x0, x1d)
        barrier()
        kq = attn_a()
        barrier()
        attn_b(kq)
        barrier()
        ffn_phase("f3", x2d, out_e)

        w.wait("gpsimd", "xos0", w.n["xos0"])
        w.wait("gpsimd", "xos1", w.n["xos1"])

    with nc.Block() as block:
        state = {"first": True}

        def mk(name):
            def _run(eng):
                schedule(Walk(name, eng, sems), state["first"])
                state["first"] = False
            return _run

        block.sync(mk("sync"))
        block.gpsimd(mk("gpsimd"))
        block.tensor(mk("tensor"))
        block.scalar(mk("scalar"))
        block.vector(mk("vector"))

    st.close()
    return nc


def kernel(**inputs):
    import ml_dtypes
    x = np.ascontiguousarray(np.asarray(inputs["L0_attn_x"], dtype=np.float32))
    flat = x.reshape(8, 4096, 2048)
    names = {
        "Wq": "L0_attn_Wq", "Wk": "L0_attn_Wk", "Wv": "L0_attn_Wv", "Wo": "L0_attn_Wo",
        "router_w": "L0_ffn_router_w", "gate_w": "L0_ffn_gate_w",
        "up_w": "L0_ffn_up_w", "down_w": "L0_ffn_down_w",
        "s_gate_w": "L0_ffn_s_gate_w", "s_gate_b": "L0_ffn_s_gate_b",
        "s_up_w": "L0_ffn_s_up_w", "s_up_b": "L0_ffn_s_up_b",
        "s_down_w": "L0_ffn_s_down_w", "s_down_b": "L0_ffn_s_down_b",
    }
    weights = {k: np.ascontiguousarray(np.asarray(inputs[v], dtype=np.float32))
               for k, v in names.items()}
    idf = np.eye(128, dtype=np.float32)
    idfb = np.eye(128, dtype=np.float32).astype(ml_dtypes.bfloat16)
    triU = np.triu(np.ones((128, 128), dtype=np.float32), 1)
    iotaF = np.tile(np.arange(128, dtype=np.float32), (128, 1))
    iota2 = np.zeros((128, 2 * TT), dtype=np.float32)
    for t in range(TT):
        iota2[:, 2 * t] = np.arange(128)
        iota2[:, 2 * t + 1] = 128 * t
    iota2 = np.ascontiguousarray(iota2.astype(ml_dtypes.bfloat16))
    eCb = np.tile((np.arange(8, dtype=np.float32) * C)[None, :], (128, 1))
    in_maps = []
    for c in range(8):
        m = dict(weights)
        m["x0"] = np.ascontiguousarray(flat[c])
        m["idf"] = idf
        m["idfb"] = idfb
        m["triU"] = np.ascontiguousarray(triU)
        m["iotaF"] = np.ascontiguousarray(iotaF)
        m["iota2"] = iota2
        m["eCb"] = np.ascontiguousarray(eCb)
        m["maskv"] = np.full((128, 1), 1.0 if c % 2 == 1 else 0.0, dtype=np.float32)
        in_maps.append(m)
    nc = build_nc()
    res = run_bass_kernel_spmd(nc, in_maps, core_ids=list(range(8)))
    out = np.concatenate([r["out"] for r in res.results], axis=0)
    return out.reshape(4, 8192, 2048)


# version 11 (final)
# speedup vs baseline: 1.0317x; 1.0317x over previous
import numpy as np
from contextlib import ExitStack

import concourse.bass as bass
import concourse.mybir as mybir
from concourse.bass_utils import run_bass_kernel_spmd

F32 = mybir.dt.float32
BF16 = mybir.dt.bfloat16
U32 = mybir.dt.uint32
AF = mybir.ActivationFunctionType
ALU = mybir.AluOpType
AX = mybir.AxisListType
IOffs = bass.IndirectOffsetOnAxis

D = 2048
KT = 16
CH = 1024
TT = 8
NCH = 4
ITL = 11
SITL = 22
C = 384
NSB = 24
BIG = 4096.0
EPS = 1e-5
DENOM = float(np.sqrt(2048.0) / 8192.0)

_NCH = NCH
_NEB = 16

ENGS = ("sync", "gpsimd", "tensor", "scalar", "vector")

SEM_NAMES = (
    ["p", "v", "a", "cc"]
    + ["xld0", "xld1", "xnb0", "xnb1", "xg0", "xg1", "xg2", "xg3"]
    + ["yev0", "yev1"]
    + ["yc0", "yc1", "yc2", "yc3", "xos0", "xos1"]
    + ["wgu0", "wgu1", "wgu2", "wdw0", "wdw1", "wdw2", "dsm"]
    + ["wgc0", "wgc1", "wgc2", "wdc0", "wdc1", "wdc2"]
    + ["wgf0", "wgf1", "wgf2", "wdf0", "wdf1", "wdf2"]
    + ["din0", "din1", "wst0", "wst1"]
    + ["wq0", "wq1", "wk0", "wk1", "wv0", "wv1", "wo0", "wo1"]
    + ["qst", "kvst0", "kvst1", "qld", "kvld", "kvld2", "ccin", "ccout", "misc"]
    + ["xin0", "xin1", "xin2", "xin3", "xst0", "xst1", "xst2", "xst3"]
)


class Walk:
    def __init__(self, name, eng, sems):
        self.name = name
        self.E = eng
        self.sems = sems
        self.n = dict.fromkeys(sems, 0)
        self.lastw = {}

    def me(self, nm):
        return self.name == nm

    def fire(self, ins, s, k=1):
        self.n[s] += k
        if ins is not None:
            ins.then_inc(self.sems[s], k)
        return self.n[s]

    def wait(self, who, s, val):
        if val <= 0 or self.name != who:
            return
        if self.lastw.get(s, 0) >= val:
            return
        self.lastw[s] = val
        self.E.wait_ge(self.sems[s], val)


def build_nc():
    nc = bass.Bass("TRN2", target_bir_lowering=False)

    x0 = nc.dram_tensor("x0", [4096, D], F32, kind="ExternalInput").ap()
    Wq = nc.dram_tensor("Wq", [D, D], F32, kind="ExternalInput").ap()
    Wk = nc.dram_tensor("Wk", [D, D], F32, kind="ExternalInput").ap()
    Wv = nc.dram_tensor("Wv", [D, D], F32, kind="ExternalInput").ap()
    Wo = nc.dram_tensor("Wo", [D, D], F32, kind="ExternalInput").ap()
    router = nc.dram_tensor("router_w", [8, D], F32, kind="ExternalInput").ap()
    gate_w = nc.dram_tensor("gate_w", [8, D, 1408], F32, kind="ExternalInput").ap()
    up_w = nc.dram_tensor("up_w", [8, D, 1408], F32, kind="ExternalInput").ap()
    down_w = nc.dram_tensor("down_w", [8, 1408, D], F32, kind="ExternalInput").ap()
    s_gate_w = nc.dram_tensor("s_gate_w", [D, 2816], F32, kind="ExternalInput").ap()
    s_gate_b = nc.dram_tensor("s_gate_b", [2816], F32, kind="ExternalInput").ap()
    s_up_w = nc.dram_tensor("s_up_w", [D, 2816], F32, kind="ExternalInput").ap()
    s_up_b = nc.dram_tensor("s_up_b", [2816], F32, kind="ExternalInput").ap()
    s_down_w = nc.dram_tensor("s_down_w", [2816, D], F32, kind="ExternalInput").ap()
    s_down_b = nc.dram_tensor("s_down_b", [D], F32, kind="ExternalInput").ap()
    idf = nc.dram_tensor("idf", [128, 128], F32, kind="ExternalInput").ap()
    idfb = nc.dram_tensor("idfb", [128, 128], BF16, kind="ExternalInput").ap()
    triU = nc.dram_tensor("triU", [128, 128], F32, kind="ExternalInput").ap()
    iotaF = nc.dram_tensor("iotaF", [128, 128], F32, kind="ExternalInput").ap()
    iota2 = nc.dram_tensor("iota2", [128, 2 * TT], BF16, kind="ExternalInput").ap()
    eCbd = nc.dram_tensor("eCb", [128, 8], F32, kind="ExternalInput").ap()
    maskv = nc.dram_tensor("maskv", [128, 1], F32, kind="ExternalInput").ap()
    out_e = nc.dram_tensor("out", [4096, D], F32, kind="ExternalOutput").ap()

    x1d = nc.dram_tensor("x1d", [4096, D], F32, kind="Internal").ap()
    x2d = nc.dram_tensor("x2d", [4096, D], F32, kind="Internal").ap()
    xnd = [nc.dram_tensor("xnd%d" % c_, [CH, D], BF16, kind="Internal").ap()
           for c_ in range(NCH)]
    yd = [nc.dram_tensor("yd%d" % c_, [8 * C + CH, D], BF16, kind="Internal").ap()
          for c_ in range(NCH)]
    qTd = nc.dram_tensor("qTd", [NCH, 128, KT * CH], BF16, kind="Internal").ap()
    kvtd = nc.dram_tensor("kvtd", [KT, 128, 4096], F32, kind="Internal").ap()
    wtd = nc.dram_tensor("wtd", [3, 128, 16, 16, 128], BF16, kind="Internal").ap()
    gwc = nc.dram_tensor("gwc", [8, ITL, 128, KT * 128], BF16, kind="Internal").ap()
    uwc = nc.dram_tensor("uwc", [8, ITL, 128, KT * 128], BF16, kind="Internal").ap()
    dwc = nc.dram_tensor("dwc", [8, 4, 128, ITL * 512], BF16, kind="Internal").ap()
    sgwc = nc.dram_tensor("sgwc", [SITL, 128, KT * 128], BF16, kind="Internal").ap()
    suwc = nc.dram_tensor("suwc", [SITL, 128, KT * 128], BF16, kind="Internal").ap()
    sdwc = nc.dram_tensor("sdwc", [4, 2, 128, ITL * 512], BF16, kind="Internal").ap()
    wod = nc.dram_tensor("wod", [128, 16, 16, 128], BF16, kind="Internal").ap()
    cc_in = nc.dram_tensor("cc_in", [128, 16], F32, kind="Internal").ap()
    cc_out = nc.dram_tensor("cc_out", [256, 16], F32, kind="Internal").ap()

    st = ExitStack()
    e = st.enter_context

    stat = e(nc.sbuf_tensor("stat", [128, 64], F32))
    mx = e(nc.sbuf_tensor("mx", [128, TT * 8], F32))
    dd = e(nc.sbuf_tensor("dd", [128, TT * 2], F32))
    ww = e(nc.sbuf_tensor("ww", [128, TT * 2], F32))
    w1p = e(nc.sbuf_tensor("w1p", [128, TT * 2], F32))
    logit = e(nc.sbuf_tensor("logit", [128, 8], F32))
    msk1 = e(nc.sbuf_tensor("msk1", [128, TT * 8], F32))
    msk2 = e(nc.sbuf_tensor("msk2", [128, TT * 8], F32))
    mskS = e(nc.sbuf_tensor("mskS", [128, TT * 8], F32))
    r_e = e(nc.sbuf_tensor("r_e", [128, TT * 8], F32))
    dest = e(nc.sbuf_tensor("dest", [128, TT * 8], F32))
    dgt = e(nc.sbuf_tensor("dgt", [128, 8], F32))
    sgt = e(nc.sbuf_tensor("sgt", [128, 16], F32))
    amask = e(nc.sbuf_tensor("amask", [128, 8], F32))
    dmm = e(nc.sbuf_tensor("dmm", [128, 8], F32))
    dsh = e(nc.sbuf_tensor("dsh", [128, NSB], F32))
    slotsel = e(nc.sbuf_tensor("slotsel", [128, TT * 2], F32))
    slotu = e(nc.sbuf_tensor("slotu", [128, TT * 2], U32))
    gtokf = e(nc.sbuf_tensor("gtokf", [128, NSB], F32))
    gtmp = e(nc.sbuf_tensor("gtmp", [128, 2 * NSB], F32))
    idxu = e(nc.sbuf_tensor("idxu", [128, NSB], U32))
    totr = e(nc.sbuf_tensor("totr", [1, TT * 8], F32))
    baser = e(nc.sbuf_tensor("baser", [1, TT * 8], F32))
    rtT = e(nc.sbuf_tensor("rtT", [128, 128], BF16))
    sgbc = e(nc.sbuf_tensor("sgbc", [128, 22], F32))
    subc = e(nc.sbuf_tensor("subc", [128, 22], F32))
    sdbr = e(nc.sbuf_tensor("sdbr", [1, D], BF16))
    ones_r = e(nc.sbuf_tensor("ones_r", [1, 128], BF16))
    onescol = e(nc.sbuf_tensor("onescol", [128, 1], F32))
    onesrow = e(nc.sbuf_tensor("onesrow", [1, 128], F32))
    carry = e(nc.sbuf_tensor("carry", [128, 16], F32))
    Tpart = e(nc.sbuf_tensor("Tpart", [128, 16], F32))
    Spart = e(nc.sbuf_tensor("Spart", [128, 16], F32))
    mask_sb = e(nc.sbuf_tensor("mask_sb", [128, 1], F32))
    eps_col = e(nc.sbuf_tensor("eps_col", [128, 1], F32))
    idf_sb = e(nc.sbuf_tensor("idf_sb", [128, 128], F32))
    idfb_sb = e(nc.sbuf_tensor("idfb_sb", [128, 128], BF16))
    triU_sb = e(nc.sbuf_tensor("triU_sb", [128, 128], F32))
    iotaF_sb = e(nc.sbuf_tensor("iotaF_sb", [128, 128], F32))
    iota2_sb = e(nc.sbuf_tensor("iota2_sb", [128, 2 * TT], BF16))
    eCb_sb = e(nc.sbuf_tensor("eCb_sb", [128, 8], F32))
    oh = e(nc.sbuf_tensor("oh", [128, NSB * 128], BF16))
    sqb = e(nc.sbuf_tensor("sqb", [128, D], BF16))

    ps_gu = e(nc.psum_tensor("ps_gu", [128, 2048], F32))
    ps_dn = e(nc.psum_tensor("ps_dn", [128, 1024], F32))

    sems = {nm: e(nc.semaphore(nm)) for nm in SEM_NAMES}

    cache = {}

    def phase_buffers(key, factory):
        if key not in cache:
            pst_ = ExitStack()
            cache[key] = [pst_, factory(pst_), False]
        return cache[key][1]

    def phase_free(key, first_walk):
        if first_walk and key in cache and not cache[key][2]:
            cache[key][0].close()
            cache[key][2] = True

    kCH = lambda k: k * CH

    def schedule(w: Walk, first_walk):
        SY, GP, TE, SC, VE = (w.me("sync"), w.me("gpsimd"), w.me("tensor"),
                              w.me("scalar"), w.me("vector"))
        E = w.E

        bankfree = {b: ("v", 0) for b in range(8)}

        def bank_wait(banks, who="tensor"):
            for b in banks:
                s_, val = bankfree[b]
                w.wait(who, s_, val)

        def bank_set(banks, s_, val):
            for b in banks:
                bankfree[b] = (s_, val)

        def barrier():
            vals = dict(w.n)
            for who in ENGS:
                for s_, val in vals.items():
                    w.wait(who, s_, val)

        pro = phase_buffers("pro", lambda ps: dict(
            pst=ps.enter_context(nc.psum_tensor("pstP", [128, 2048], BF16)),
            wtmp=ps.enter_context(nc.sbuf_tensor("wtmp", [128, 2 * D], F32, side="right")),
            wbf=ps.enter_context(nc.sbuf_tensor("wbf", [128, 2 * D], BF16, side="right")),
            stg=ps.enter_context(nc.sbuf_tensor("stg", [128, 2 * D], BF16, side="right")),
            rtmp=ps.enter_context(nc.sbuf_tensor("rtmp", [8, D], F32, side="right")),
            rbf=ps.enter_context(nc.sbuf_tensor("rbf", [8, D], BF16, side="right")),
            btmp=ps.enter_context(nc.sbuf_tensor("btmp", [22, 2 * 128], F32, side="right")),
            sdbt=ps.enter_context(nc.sbuf_tensor("sdbt", [1, D], F32, side="right")),
        ))
        wtmp, wbf, stg, pstP = pro["wtmp"], pro["wbf"], pro["stg"], pro["pst"]

        ins = E.dma_start(out=idf_sb[:], in_=idf) if SY else None
        w.fire(ins, "dsm", 16)
        ins = E.dma_start(out=idfb_sb[:], in_=idfb) if SY else None
        w.fire(ins, "dsm", 16)
        ins = E.dma_start(out=triU_sb[:], in_=triU) if SY else None
        w.fire(ins, "dsm", 16)
        ins = E.dma_start(out=iotaF_sb[:], in_=iotaF) if SY else None
        w.fire(ins, "dsm", 16)
        ins = E.dma_start(out=iota2_sb[:], in_=iota2) if SY else None
        w.fire(ins, "dsm", 16)
        ins = E.dma_start(out=eCb_sb[:], in_=eCbd) if SY else None
        w.fire(ins, "dsm", 16)
        ins = E.dma_start(out=mask_sb[:], in_=maskv) if SY else None
        w.fire(ins, "dsm", 16)
        ins = E.dma_start(out=pro["rtmp"][:], in_=router) if SY else None
        w.fire(ins, "dsm", 16)
        ins = E.dma_start(out=pro["btmp"][:, 0:128],
                          in_=s_gate_b.rearrange("(t p) -> t p", p=128)) if SY else None
        w.fire(ins, "dsm", 16)
        ins = E.dma_start(out=pro["btmp"][:, 128:256],
                          in_=s_up_b.rearrange("(t p) -> t p", p=128)) if SY else None
        w.fire(ins, "dsm", 16)
        ins = E.dma_start(out=pro["sdbt"][:],
                          in_=s_down_b.rearrange("(a d) -> a d", a=1)) if SY else None
        v_small = w.fire(ins, "dsm", 16)

        for tgt, val in ((ones_r, 1.0), (onescol, 1.0), (onesrow, 1.0),
                         (eps_col, EPS), (carry, 0.0)):
            ins = E.memset(tgt[:], val) if GP else None
            v_msets = w.fire(ins, "misc", 1)

        pro_cast_v = [0, 0]
        pro_tr_b = [0, 0]
        pro_ev_b = [0, 0]
        pro_st_slot = [0, 0]
        for m, src in enumerate([Wq, Wk, Wv, Wo]):
            for eb in range(_NEB):
                slot = eb % 2
                w.wait("sync", "v", pro_cast_v[slot])
                ins = E.dma_start(out=wtmp[:, slot * D:(slot + 1) * D],
                                  in_=src[eb * 128:(eb + 1) * 128, :]) if SY else None
                v_ld = w.fire(ins, "din%d" % slot, 16)
                w.wait("vector", "din%d" % slot, v_ld)
                w.wait("vector", "p", pro_tr_b[0])
                w.wait("vector", "p", pro_tr_b[1])
                ins = E.tensor_copy(out=wbf[:, slot * D:(slot + 1) * D],
                                    in_=wtmp[:, slot * D:(slot + 1) * D]) if VE else None
                v_cast = w.fire(ins, "v", 1)
                pro_cast_v[slot] = v_cast
                if m == 0 and eb == 0:
                    w.wait("tensor", "dsm", v_small)
                w.wait("tensor", "v", v_cast)
                for half in range(2):
                    bank_wait([6 + half])
                    last = None
                    for j in range(8):
                        k = half * 8 + j
                        if TE:
                            last = E.transpose(
                                pstP[:, half * 1024 + j * 128: half * 1024 + (j + 1) * 128],
                                wbf[:, slot * D + k * 128: slot * D + (k + 1) * 128],
                                idfb_sb[:])
                    p_tr = w.fire(last, "p", 1)
                    pro_tr_b[half] = p_tr
                    w.wait("vector", "p", p_tr)
                    w.wait("vector", "wst%d" % slot, pro_st_slot[slot])
                    if VE:
                        if m == 3:
                            ins = E.tensor_scalar(
                                out=stg[:, slot * D + half * 1024: slot * D + (half + 1) * 1024],
                                in0=pstP[:, half * 1024:(half + 1) * 1024],
                                scalar1=DENOM, scalar2=None, op0=ALU.mult)
                        else:
                            ins = E.tensor_copy(
                                out=stg[:, slot * D + half * 1024: slot * D + (half + 1) * 1024],
                                in_=pstP[:, half * 1024:(half + 1) * 1024])
                    else:
                        ins = None
                    v_ev = w.fire(ins, "v", 1)
                    pro_ev_b[half] = v_ev
                    bank_set([6 + half], "v", v_ev)
                w.wait("scalar", "v", pro_ev_b[0])
                w.wait("scalar", "v", pro_ev_b[1])
                dst = wtd[m][:, eb] if m < 3 else wod[:, :, eb, :]
                ins = E.dma_start(
                    out=dst,
                    in_=stg[:, slot * D:(slot + 1) * D].rearrange(
                        "p (k n) -> p k n", k=16)) if SC else None
                pro_st_slot[slot] = w.fire(ins, "wst%d" % slot, 16)

        w.wait("vector", "dsm", v_small)
        ins = E.tensor_copy(out=pro["rbf"][:], in_=pro["rtmp"][:]) if VE else None
        v_rbf = w.fire(ins, "v", 1)
        w.wait("tensor", "v", v_rbf)
        bank_wait([6])
        last = None
        for k in range(16):
            if TE:
                last = E.transpose(pstP[:, k * 8:(k + 1) * 8],
                                   pro["rbf"][:, k * 128:(k + 1) * 128],
                                   idfb_sb[0:8, 0:8])
        p_rt = w.fire(last, "p", 1)
        w.wait("vector", "p", p_rt)
        ins = E.tensor_copy(out=rtT[:, 0:128], in_=pstP[:, 0:128]) if VE else None
        v_rt = w.fire(ins, "v", 1)
        bank_set([6], "v", v_rt)

        bank_wait([0])
        if TE:
            E.transpose(ps_gu[:, 0:22], pro["btmp"][:, 0:128], idf_sb[0:22, 0:22])
            ins = E.transpose(ps_gu[:, 32:54], pro["btmp"][:, 128:256],
                              idf_sb[0:22, 0:22])
        p_b = w.fire(ins if TE else None, "p", 1)
        w.wait("vector", "p", p_b)
        if VE:
            E.tensor_copy(out=sgbc[:], in_=ps_gu[:, 0:22])
            ins = E.tensor_copy(out=subc[:], in_=ps_gu[:, 32:54])
        v_bc = w.fire(ins if VE else None, "v", 1)
        bank_set([0], "v", v_bc)
        w.wait("vector", "dsm", v_small)
        ins = E.tensor_copy(out=sdbr[:], in_=pro["sdbt"][:]) if VE else None
        w.fire(ins, "v", 1)
        w.wait("vector", "misc", v_msets)

        phase_free("pro", first_walk)
        barrier()

        def fresh_ln_state():
            return dict(xld_gate={0: [], 1: []}, xnb_gate={0: [], 1: []},
                        a_sq={}, v_xn={})

        def ln_stage(tb, ls, c, in_ap, with_router, pst):
            xld, xnb, xnT = tb["xld"], tb["xnb"], tb["xnT"]
            v_xnT_last = 0
            for t in range(TT):
                tok0 = c * CH + t * 128
                sl = t % 2
                xsl = xld[:, sl * D:(sl + 1) * D]
                for (s_, val) in ls["xld_gate"][sl]:
                    w.wait("sync", s_, val)
                ins = E.dma_start(out=xsl, in_=in_ap[tok0:tok0 + 128, :]) if SY else None
                v_xin = w.fire(ins, "xld%d" % sl, 16)

                cS = stat[:, t * 8 + 0:t * 8 + 1]
                cQ = stat[:, t * 8 + 1:t * 8 + 2]
                cM = stat[:, t * 8 + 2:t * 8 + 3]
                cT = stat[:, t * 8 + 3:t * 8 + 4]
                cNV = stat[:, t * 8 + 4:t * 8 + 5]
                cL = stat[:, t * 8 + 5:t * 8 + 6]
                cR = stat[:, t * 8 + 6:t * 8 + 7]
                xnsl = xnb[:, sl * D:(sl + 1) * D]

                w.wait("vector", "xld%d" % sl, v_xin)
                ins = E.reduce_sum(cS, xsl, axis=AX.X) if VE else None
                v_rs = w.fire(ins, "v", 1)
                w.wait("scalar", "xld%d" % sl, v_xin)
                ins = E.activation(out=sqb[:], in_=xsl, func=AF.Square,
                                   accum_out=cQ) if SC else None
                a_sq = w.fire(ins, "a", 1)
                w.wait("vector", "a", a_sq)
                w.wait("vector", "v", v_rs)
                if VE:
                    E.tensor_scalar(out=cM, in0=cS, scalar1=1.0 / D, scalar2=None,
                                    op0=ALU.mult)
                    ins = E.tensor_scalar(out=cT, in0=cQ, scalar1=1.0 / D,
                                          scalar2=None, op0=ALU.mult)
                v_m = w.fire(ins if VE else None, "v", 1)
                w.wait("vector", "v", v_m)
                ins = E.scalar_tensor_tensor(out=cNV, in0=cM, scalar=cM, in1=cT,
                                             op0=ALU.mult, op1=ALU.subtract) if VE else None
                v_nv = w.fire(ins, "v", 1)
                w.wait("scalar", "v", v_nv)
                ins = E.activation(out=cL, in_=cNV, func=AF.Ln, bias=eps_col[:],
                                   scale=-1.0) if SC else None
                a_l = w.fire(ins, "a", 1)
                w.wait("scalar", "a", a_l)
                ins = E.activation(out=cR, in_=cL, func=AF.Exp, scale=-0.5) if SC else None
                a_r = w.fire(ins, "a", 1)
                w.wait("vector", "a", a_r)
                for (s_, val) in ls["xnb_gate"][sl]:
                    w.wait("vector", s_, val)
                ins = E.tensor_scalar(out=xnsl, in0=xsl, scalar1=cM, scalar2=cR,
                                      op0=ALU.subtract, op1=ALU.mult) if VE else None
                v_xn = w.fire(ins, "v", 1)
                ls["a_sq"][t] = a_sq
                ls["v_xn"][t] = v_xn
                ls["xld_gate"][sl] = [("a", a_sq), ("v", v_xn)]

                if tb.get("store_xnd"):
                    w.wait("sync", "v", v_xn)
                    ins = E.dma_start(out=xnd[c][t * 128:(t + 1) * 128, :],
                                      in_=xnsl) if SY else None
                    v_xst = w.fire(ins, "xnb%d" % sl, 16)
                    ls.setdefault("xnst", {})[t] = v_xst
                    xnb_gate_extra = [("xnb%d" % sl, v_xst)]
                else:
                    xnb_gate_extra = []

                w.wait("tensor", "v", v_xn)
                p_tr_halves = []
                for half in range(2):
                    bank_wait([6 + half])
                    last = None
                    for j in range(8):
                        k = half * 8 + j
                        if TE:
                            last = E.transpose(
                                pst[:, half * 1024 + j * 128: half * 1024 + (j + 1) * 128],
                                xnsl[:, k * 128:(k + 1) * 128], idfb_sb[:])
                    p_tr = w.fire(last, "p", 1)
                    p_tr_halves.append(p_tr)
                    w.wait("vector", "p", p_tr)
                    ins = E.tensor_copy(
                        out=xnT[:].rearrange("p (k n) -> p k n", k=KT)[
                            :, half * 8:(half + 1) * 8, t * 128:(t + 1) * 128],
                        in_=pst[:, half * 1024:(half + 1) * 1024].rearrange(
                            "p (k n) -> p k n", k=8)) if VE else None
                    v_xnT = w.fire(ins, "v", 1)
                    bank_set([6 + half], "v", v_xnT)
                    v_xnT_last = v_xnT
                ls["xnb_gate"][sl] = xnb_gate_extra + [("p", p_tr_halves[1])]

                if with_router:
                    w.wait("tensor", "v", v_xnT_last)
                    lg_off = (t % 2) * 1024
                    bank_wait([0 + (t % 2) * 2])
                    last = None
                    for k in range(KT):
                        if TE:
                            last = E.matmul(
                                ps_gu[:, lg_off + 0:lg_off + 8],
                                xnT[:, kCH(k) + t * 128: kCH(k) + (t + 1) * 128],
                                rtT[:, k * 8:(k + 1) * 8],
                                start=(k == 0), stop=(k == KT - 1))
                    p_lg = w.fire(last, "p", 1)
                    w.wait("vector", "p", p_lg)
                    ins = E.tensor_copy(out=logit[:], in_=ps_gu[:, lg_off:lg_off + 8]) if VE else None
                    v_lg = w.fire(ins, "v", 1)
                    w.wait("vector", "v", v_lg)
                    ins = E.max(out=mx[:, t * 8:(t + 1) * 8], in_=logit[:]) if VE else None
                    v_mx = w.fire(ins, "v", 1)
                    w.wait("vector", "v", v_mx)
                    m1 = mx[:, t * 8 + 0:t * 8 + 1]
                    m2 = mx[:, t * 8 + 1:t * 8 + 2]
                    if VE:
                        E.tensor_sub(dd[:, t * 2 + 0:t * 2 + 1], m1, m2)
                        ins = E.tensor_sub(dd[:, t * 2 + 1:t * 2 + 2], m2, m1)
                    v_dd = w.fire(ins if VE else None, "v", 1)
                    w.wait("scalar", "v", v_dd)
                    ins = E.activation(out=ww[:, t * 2:t * 2 + 2],
                                       in_=dd[:, t * 2:t * 2 + 2],
                                       func=AF.Sigmoid) if SC else None
                    a_w = w.fire(ins, "a", 1)
                    w.wait("vector", "a", a_w)
                    ins = E.tensor_scalar(out=w1p[:, t * 2:t * 2 + 2],
                                          in0=ww[:, t * 2:t * 2 + 2], scalar1=0.5,
                                          scalar2=None, op0=ALU.mult) if VE else None
                    w.fire(ins, "v", 1)
                    if VE:
                        E.tensor_scalar(out=msk1[:, t * 8:(t + 1) * 8], in0=logit[:],
                                        scalar1=m1, scalar2=None, op0=ALU.is_equal)
                        ins = E.tensor_scalar(out=msk2[:, t * 8:(t + 1) * 8], in0=logit[:],
                                              scalar1=m2, scalar2=None, op0=ALU.is_equal)
                    v_mk = w.fire(ins if VE else None, "v", 1)
                    w.wait("vector", "v", v_mk)
                    ins = E.tensor_add(mskS[:, t * 8:(t + 1) * 8],
                                       msk1[:, t * 8:(t + 1) * 8],
                                       msk2[:, t * 8:(t + 1) * 8]) if VE else None
                    v_ms = w.fire(ins, "v", 1)
                    ls.setdefault("v_msk", {})[t] = v_ms
                    w.wait("tensor", "v", v_ms)
                    rk_off = (t % 2) * 1024 + 16
                    last = E.matmul(ps_gu[:, rk_off:rk_off + 8], triU_sb[:],
                                    mskS[:, t * 8:(t + 1) * 8],
                                    start=True, stop=True) if TE else None
                    p_rk = w.fire(last, "p", 1)
                    w.wait("vector", "p", p_rk)
                    ins = E.tensor_copy(out=r_e[:, t * 8:(t + 1) * 8],
                                        in_=ps_gu[:, rk_off:rk_off + 8]) if VE else None
                    v_re = w.fire(ins, "v", 1)
                    bank_set([0 + (t % 2) * 2], "v", v_re)
            return v_xnT_last

        def routing_finalize(ls):
            w.wait("tensor", "v", ls["v_msk"][TT - 1])
            bank_wait([4])
            last = E.matmul(ps_dn[0:1, 0:64], onescol[:, 0:1], mskS[:, 0:64],
                            start=True, stop=True) if TE else None
            p_tot = w.fire(last, "p", 1)
            w.wait("vector", "p", p_tot)
            ins = E.tensor_copy(out=totr[0:1, :], in_=ps_dn[0:1, 0:64]) if VE else None
            v_tot = w.fire(ins, "v", 1)
            bank_set([4], "v", v_tot)
            ins = E.memset(baser[0:1, 0:8], 0.0) if VE else None
            v_ch = w.fire(ins, "v", 1)
            for t in range(1, TT):
                w.wait("vector", "v", v_ch)
                ins = E.tensor_add(baser[0:1, t * 8:(t + 1) * 8],
                                   baser[0:1, (t - 1) * 8:t * 8],
                                   totr[0:1, (t - 1) * 8:t * 8]) if VE else None
                v_ch = w.fire(ins, "v", 1)
            w.wait("tensor", "v", v_ch)
            bank_wait([5])
            last = E.matmul(ps_dn[:, 512:576], onesrow[0:1, :], baser[0:1, 0:64],
                            start=True, stop=True) if TE else None
            p_bc = w.fire(last, "p", 1)

            gtok_first = [True]
            p_gtok_tile = {}
            v_bank5_last = 0
            for t in range(TT):
                w.wait("vector", "p", p_bc)
                ins = E.tensor_add(dest[:, t * 8:(t + 1) * 8],
                                   r_e[:, t * 8:(t + 1) * 8],
                                   ps_dn[:, 512 + t * 8:512 + (t + 1) * 8]) if VE else None
                v_d = w.fire(ins, "v", 1)
                w.wait("vector", "v", v_d)
                if VE:
                    E.tensor_add(dgt[:], dest[:, t * 8:(t + 1) * 8], eCb_sb[:])
                    ins = E.tensor_scalar(out=amask[:], in0=mskS[:, t * 8:(t + 1) * 8],
                                          scalar1=-BIG, scalar2=BIG,
                                          op0=ALU.mult, op1=ALU.add)
                v_a = w.fire(ins if VE else None, "v", 1)
                w.wait("vector", "v", v_a)
                if VE:
                    E.tensor_mul(sgt[:, 0:8], dgt[:], msk1[:, t * 8:(t + 1) * 8])
                    E.tensor_mul(sgt[:, 8:16], dgt[:], msk2[:, t * 8:(t + 1) * 8])
                    ins = E.tensor_mul(dmm[:], dest[:, t * 8:(t + 1) * 8],
                                       mskS[:, t * 8:(t + 1) * 8])
                v_s = w.fire(ins if VE else None, "v", 1)
                w.wait("vector", "v", v_s)
                if VE:
                    E.reduce_sum(slotsel[:, t * 2 + 0:t * 2 + 1], sgt[:, 0:8], axis=AX.X)
                    E.reduce_sum(slotsel[:, t * 2 + 1:t * 2 + 2], sgt[:, 8:16], axis=AX.X)
                    ins = E.tensor_add(dmm[:], dmm[:], amask[:])
                v_dm = w.fire(ins if VE else None, "v", 1)
                w.wait("vector", "v", v_dm)
                ins = None
                for s_ in range(3):
                    if VE:
                        ins = E.tensor_scalar(
                            out=dsh.rearrange("p (e s) -> p e s", s=3)[:, :, s_],
                            in0=dmm[:], scalar1=float(128 * s_), scalar2=None,
                            op0=ALU.subtract)
                v_dsh = w.fire(ins if VE else None, "v", 1)
                w.wait("vector", "v", v_dsh)
                if t > 0:
                    w.wait("vector", "p", p_gtok_tile[t - 1])
                ins = None
                for b in range(NSB):
                    if VE:
                        ins = E.tensor_scalar(out=oh[:, b * 128:(b + 1) * 128],
                                              in0=iotaF_sb[:],
                                              scalar1=dsh[:, b:b + 1],
                                              scalar2=None, op0=ALU.is_equal)
                v_oh = w.fire(ins if VE else None, "v", 1)
                w.wait("tensor", "v", v_oh)
                if gtok_first[0]:
                    bank_wait([1])
                last = None
                for b in range(NSB):
                    if TE:
                        last = E.matmul(ps_gu[:, 512 + 2 * b:512 + 2 * b + 2],
                                        oh[:, b * 128:(b + 1) * 128],
                                        iota2_sb[:, 2 * t:2 * t + 2],
                                        start=gtok_first[0],
                                        stop=(t == TT - 1 and b == NSB - 1),
                                        skip_group_check=True)
                    gtok_first[0] = False
                p_gt = w.fire(last, "p", 1)
                p_gtok_tile[t] = p_gt
            bank_set([5], "v", w.n["v"])

            w.wait("vector", "p", p_gtok_tile[TT - 1])
            ins = E.tensor_copy(out=gtmp[:], in_=ps_gu[:, 512:512 + 2 * NSB]) if VE else None
            v_g1 = w.fire(ins, "v", 1)
            bank_set([1], "v", v_g1)
            w.wait("vector", "v", v_g1)
            ins = E.tensor_add(gtokf[:],
                               gtmp[:].rearrange("p (b two) -> p b two", two=2)[:, :, 0],
                               gtmp[:].rearrange("p (b two) -> p b two", two=2)[:, :, 1]) if VE else None
            v_g2 = w.fire(ins, "v", 1)
            w.wait("vector", "v", v_g2)
            if VE:
                E.tensor_copy(out=idxu[:], in_=gtokf[:])
                ins = E.tensor_copy(out=slotu[:], in_=slotsel[:])
            v_idx = w.fire(ins if VE else None, "v", 1)
            return v_idx

        def ffn_phase(key, in_ap, out_ap):
            tb = phase_buffers(key, lambda ps: dict(
                pst=ps.enter_context(nc.psum_tensor(key + "pst", [128, 2048], BF16)),
                xld=ps.enter_context(nc.sbuf_tensor(key + "xld", [128, 2 * D], F32, side="right")),
                xnb=ps.enter_context(nc.sbuf_tensor(key + "xnb", [128, 2 * D], BF16, side="right")),
                xnT=ps.enter_context(nc.sbuf_tensor(key + "xnT", [128, KT * CH], BF16, side="right")),
                xg=ps.enter_context(nc.sbuf_tensor(key + "xg", [128, 4 * D], BF16, side="right")),
                xgT=ps.enter_context(nc.sbuf_tensor(key + "xgT", [128, 2 * KT * C], BF16, side="right")),
                gT=ps.enter_context(nc.sbuf_tensor(key + "gT", [128, SITL * 512], BF16, side="right")),
                dwq=ps.enter_context(nc.sbuf_tensor(key + "dwq", [128, 3 * ITL * 512], BF16, side="right")),
                gwu=ps.enter_context(nc.sbuf_tensor(key + "gwu", [128, 3 * D], BF16, side="right")),
                silu=ps.enter_context(nc.sbuf_tensor(key + "silu", [128, 2 * 512], F32, side="right")),
                y_ev=ps.enter_context(nc.sbuf_tensor(key + "yev", [128, 2 * 512], BF16, side="right")),
                ycb=ps.enter_context(nc.sbuf_tensor(key + "ycb", [128, 4 * D], BF16, side="right")),
            ))
            tb["store_xnd"] = True
            pst = tb["pst"]
            xgT, gT, dwq, gwu, silu_b, y_ev, ycb = (tb["xgT"], tb["gT"], tb["dwq"],
                                                    tb["gwu"], tb["silu"], tb["y_ev"],
                                                    tb["ycb"])
            ls = fresh_ln_state()
            gwu_slot_p = [0, 0, 0]
            dwq_slot_p = [0, 0, 0]
            xg_slot_p = [0, 0, 0, 0]
            yev_v = [0, 0]
            yev_st = [0, 0]
            gm_slot = [0, 0]
            xgT_half_p = [0, 0]
            ycb_gate = [[] for _ in range(4)]
            xos_gate = {0: [], 1: []}

            units = [("s", 0), ("s", 1)] + [("e", ee) for ee in range(8)]

            gwc_st = [0, 0, 0]
            dwc_st = [0, 0, 0]
            cache_ready = []
            for c in range(_NCH):
                use_cache = key == "f3" or c > 0
                fill_cache = key == "f1" and c == 0
                v_xnT_all = ln_stage(tb, ls, c, in_ap, True, pst)
                v_idx = routing_finalize(ls)

                xg_ld = {}

                def emit_gather(j):
                    slot = j % 4
                    w.wait("gpsimd", "v", v_idx)
                    w.wait("gpsimd", "xnb0", ls["xnst"].get(6, 0))
                    w.wait("gpsimd", "xnb1", ls["xnst"].get(7, 0))
                    w.wait("gpsimd", "p", xg_slot_p[slot])
                    ins = E.indirect_dma_start(
                        out=tb["xg"][:, slot * D:(slot + 1) * D],
                        out_offset=None,
                        in_=xnd[c],
                        in_offset=IOffs(ap=idxu[:, j:j + 1], axis=0)) if GP else None
                    xg_ld[j] = w.fire(ins, "xg%d" % slot, 16)

                xgT_ready = {}

                def emit_transposes(ee):
                    h_ = ee % 2
                    v_last = 0
                    for s_ in range(3):
                        j = ee * 3 + s_
                        slot = j % 4
                        w.wait("tensor", "xg%d" % slot, xg_ld[j])
                        for half in range(2):
                            bank_wait([6 + half])
                            last = None
                            for jj in range(8):
                                k = half * 8 + jj
                                if TE:
                                    last = E.transpose(
                                        pst[:, half * 1024 + jj * 128:
                                            half * 1024 + (jj + 1) * 128],
                                        tb["xg"][:, slot * D + k * 128:
                                                 slot * D + (k + 1) * 128],
                                        idfb_sb[:])
                            p_tr = w.fire(last, "p", 1)
                            xg_slot_p[slot] = p_tr
                            w.wait("vector", "p", p_tr)
                            w.wait("vector", "p", xgT_half_p[h_])
                            ins = E.tensor_copy(
                                out=xgT[:, h_ * KT * C:(h_ + 1) * KT * C].rearrange(
                                    "p (k n) -> p k n", k=KT)[
                                    :, half * 8:(half + 1) * 8,
                                    s_ * 128:(s_ + 1) * 128],
                                in_=pst[:, half * 1024:(half + 1) * 1024].rearrange(
                                    "p (k n) -> p k n", k=8)) if VE else None
                            v_last = w.fire(ins, "v", 1)
                            bank_set([6 + half], "v", v_last)
                    xgT_ready[ee] = v_last

                p_gu_last = 0
                for ui, (kind, uu) in enumerate(units):
                    routed = kind == "e"
                    n_tok = C if routed else 512
                    n_it = ITL if routed else SITL
                    n_st = 3 if routed else 4
                    if routed:
                        gsrc = lambda it, ee=uu: gate_w[ee][:, it * 128:(it + 1) * 128]
                        usrc = lambda it, ee=uu: up_w[ee][:, it * 128:(it + 1) * 128]
                        dsrc_q = lambda q, ee=uu: down_w[ee][:, q * 512:(q + 1) * 512]
                        rowbase = uu * C
                    else:
                        gsrc = lambda it: s_gate_w[:, it * 128:(it + 1) * 128]
                        usrc = lambda it: s_up_w[:, it * 128:(it + 1) * 128]
                        dsrc_q = lambda q: s_down_w[:, q * 512:(q + 1) * 512]
                        rowbase = 8 * C + uu * 512

                    if ui == 1:
                        for j in range(3):
                            emit_gather(j)
                    if routed:
                        v_act_ready = xgT_ready[uu]
                        hh_ = uu % 2
                        mv = lambda k, xgT=xgT, hh_=hh_: xgT[
                            :, hh_ * KT * C + k * C: hh_ * KT * C + k * C + C]
                    else:
                        v_act_ready = v_xnT_all
                        off = uu * 512
                        mv = lambda k, off=off: tb["xnT"][:, kCH(k) + off: kCH(k) + off + 512]

                    wgu_v = {}

                    def load_gu(jj, it, which):
                        slot = jj % 3
                        if not use_cache:
                            w.wait("gpsimd", "p", gwu_slot_p[slot])
                            w.wait("gpsimd", "wgc%d" % slot, gwc_st[slot])
                            src = gsrc(it) if which == 0 else usrc(it)
                            ins = E.dma_start(
                                out=gwu[:, slot * D:(slot + 1) * D].rearrange(
                                    "p (k n) -> p k n", k=KT),
                                in_=src.rearrange("(k p) i -> p k i", p=128)) if GP else None
                            v_l = w.fire(ins, "wgf%d" % slot, 16)
                            sem_l = "wgf%d" % slot
                            if fill_cache:
                                if routed:
                                    cdst = (gwc if which == 0 else uwc)[uu, it]
                                else:
                                    cdst = (sgwc if which == 0 else suwc)[it]
                                w.wait("scalar", "wgf%d" % slot, v_l)
                                ins = E.dma_start(out=cdst,
                                                  in_=gwu[:, slot * D:(slot + 1) * D]) if SC else None
                                gwc_st[slot] = w.fire(ins, "wgc%d" % slot, 16)
                        else:
                            eng = ("sync", "gpsimd", "scalar")[slot]
                            ON = (SY, GP, SC)[slot]
                            w.wait(eng, "p", gwu_slot_p[slot])
                            for qq, vv in cache_ready:
                                w.wait(eng, qq, vv)
                            if routed:
                                csrc = (gwc if which == 0 else uwc)[uu, it]
                            else:
                                csrc = (sgwc if which == 0 else suwc)[it]
                            ins = E.dma_start(out=gwu[:, slot * D:(slot + 1) * D],
                                              in_=csrc) if ON else None
                            v_l = w.fire(ins, "wgu%d" % slot, 16)
                            sem_l = "wgu%d" % slot
                        wgu_v[jj] = (v_l, slot, sem_l)

                    load_gu(0, 0, 0)
                    load_gu(1, 0, 1)
                    load_gu(2, 1, 0)

                    dwq_v = {}

                    def load_dw(jq, q, part):
                        slot = jq % 3
                        if not use_cache:
                            w.wait("gpsimd", "p", dwq_slot_p[slot])
                            w.wait("gpsimd", "wdc%d" % slot, dwc_st[slot])
                            if routed:
                                srcap = dsrc_q(q).rearrange("(i p) n -> p i n", p=128)
                            else:
                                srcap = dsrc_q(q)[part * 1408:(part + 1) * 1408, :].rearrange(
                                    "(i p) n -> p i n", p=128)
                            ins = E.dma_start(
                                out=dwq[:, slot * ITL * 512:(slot + 1) * ITL * 512].rearrange(
                                    "p (i n) -> p i n", i=ITL),
                                in_=srcap) if GP else None
                            v_l = w.fire(ins, "wdf%d" % slot, 16)
                            sem_l = "wdf%d" % slot
                            if fill_cache:
                                cdst = dwc[uu, q] if routed else sdwc[q, part]
                                w.wait("scalar", "wdf%d" % slot, v_l)
                                ins = E.dma_start(
                                    out=cdst,
                                    in_=dwq[:, slot * ITL * 512:(slot + 1) * ITL * 512]) if SC else None
                                dwc_st[slot] = w.fire(ins, "wdc%d" % slot, 16)
                        else:
                            eng = ("scalar", "sync", "gpsimd")[slot]
                            ON = (SC, SY, GP)[slot]
                            w.wait(eng, "p", dwq_slot_p[slot])
                            for qq, vv in cache_ready:
                                w.wait(eng, qq, vv)
                            csrc = dwc[uu, q] if routed else sdwc[q, part]
                            ins = E.dma_start(
                                out=dwq[:, slot * ITL * 512:(slot + 1) * ITL * 512],
                                in_=csrc) if ON else None
                            v_l = w.fire(ins, "wdw%d" % slot, 16)
                            sem_l = "wdw%d" % slot
                        dwq_v[jq] = (v_l, slot, sem_l)

                    nparts = 1 if routed else 2
                    for jq in range(3):
                        load_dw(jq, jq // nparts, jq % nparts)

                    v_g_last = 0
                    jj = 2
                    for it in range(n_it):
                        g_v, g_slot, g_sem = wgu_v[2 * it]
                        u_v, u_slot, u_sem = wgu_v[2 * it + 1]
                        pslot = it % 2
                        g_ps = ps_gu[:, pslot * 1024 + 0: pslot * 1024 + n_tok]
                        u_ps = ps_gu[:, pslot * 1024 + 512: pslot * 1024 + 512 + n_tok]
                        gbank = [0 + pslot * 2]
                        ubank = [1 + pslot * 2]

                        w.wait("tensor", g_sem, g_v)
                        if ui == 0 and it == 0:
                            w.wait("tensor", "v", v_act_ready)
                        if routed and it == 0:
                            w.wait("tensor", "v", v_act_ready)
                        bank_wait(gbank)
                        last = None
                        for k in range(KT):
                            if TE:
                                last = E.matmul(
                                    g_ps,
                                    gwu[:, g_slot * D + k * 128: g_slot * D + (k + 1) * 128],
                                    mv(k), start=(k == 0), stop=(k == KT - 1))
                        p_g = w.fire(last, "p", 1)
                        gwu_slot_p[g_slot] = p_g
                        w.wait("tensor", u_sem, u_v)
                        bank_wait(ubank)
                        last = None
                        for k in range(KT):
                            if TE:
                                last = E.matmul(
                                    u_ps,
                                    gwu[:, u_slot * D + k * 128: u_slot * D + (k + 1) * 128],
                                    mv(k), start=(k == 0), stop=(k == KT - 1))
                        p_u = w.fire(last, "p", 1)
                        gwu_slot_p[u_slot] = p_u
                        p_gu_last = p_u
                        if routed:
                            xgT_half_p[uu % 2] = p_u
                        jj += 1
                        if jj < 2 * n_it:
                            load_gu(jj, jj // 2, jj % 2)
                            jj += 1
                        if jj < 2 * n_it:
                            load_gu(jj, jj // 2, jj % 2)

                        w.wait("scalar", "p", p_g)
                        w.wait("scalar", "v", gm_slot[pslot])
                        sl = silu_b[:, pslot * 512: pslot * 512 + n_tok]
                        if SC:
                            if routed:
                                ins = E.activation(out=sl, in_=g_ps, func=AF.Silu)
                            else:
                                ins = E.activation(out=sl, in_=g_ps, func=AF.Silu,
                                                   bias=sgbc[:, it:it + 1])
                        else:
                            ins = None
                        a_v = w.fire(ins, "a", 1)
                        bank_set(gbank, "a", a_v)

                        w.wait("vector", "a", a_v)
                        w.wait("vector", "p", p_u)
                        gsl = gT[:, it * 512: it * 512 + n_tok]
                        if VE:
                            if routed:
                                ins = E.tensor_mul(gsl, sl, u_ps)
                            else:
                                ins = E.scalar_tensor_tensor(
                                    out=gsl, in0=u_ps, scalar=subc[:, it:it + 1],
                                    in1=sl, op0=ALU.add, op1=ALU.mult)
                        else:
                            ins = None
                        v_g = w.fire(ins, "v", 1)
                        bank_set(ubank, "v", v_g)
                        gm_slot[pslot] = v_g
                        v_g_last = v_g

                    nxt = 0 if (kind == "s" and uu == 1) else (uu + 1 if routed else None)
                    if nxt is not None and nxt < 8:
                        emit_transposes(nxt)
                        if nxt + 1 < 8:
                            for j in range(3 * (nxt + 1), 3 * (nxt + 2)):
                                emit_gather(j)

                    didx = 0
                    for q in range(4):
                        base_j = q * nparts
                        needs = [(base_j + pp) for pp in range(nparts)]
                        for nj in needs:
                            if nj not in dwq_v:
                                load_dw(nj, nj // nparts, nj % nparts)
                        pf = (q + 1) * nparts
                        if pf < 4 * nparts and pf not in dwq_v:
                            load_dw(pf, pf // nparts, pf % nparts)
                        for s_ in range(n_st):
                            dn_slot = didx % 2
                            dn_ps = ps_dn[:, dn_slot * 512:(dn_slot + 1) * 512]
                            dbank = [4 + dn_slot]
                            for nj in needs:
                                w.wait("tensor", dwq_v[nj][2], dwq_v[nj][0])
                            if didx == 0:
                                w.wait("tensor", "v", v_g_last)
                            bank_wait(dbank)
                            last = None
                            for pp in range(nparts):
                                dslot = dwq_v[base_j + pp][1]
                                for it in range(ITL):
                                    git = pp * ITL + it
                                    if TE:
                                        last = E.matmul(
                                            dn_ps,
                                            gT[:, git * 512 + s_ * 128: git * 512 + (s_ + 1) * 128],
                                            dwq[:, dslot * ITL * 512 + it * 512:
                                                dslot * ITL * 512 + (it + 1) * 512],
                                            start=(pp == 0 and it == 0),
                                            stop=(pp == nparts - 1 and it == ITL - 1
                                                  and routed))
                            if not routed:
                                if TE:
                                    last = E.matmul(
                                        dn_ps, ones_r[0:1, :],
                                        sdbr[0:1, q * 512:(q + 1) * 512],
                                        start=False, stop=True)
                            p_d = w.fire(last, "p", 1)
                            if s_ == n_st - 1:
                                for nj in needs:
                                    dwq_slot_p[dwq_v[nj][1]] = p_d

                            w.wait("vector", "p", p_d)
                            ysl_ = didx % 2
                            w.wait("vector", "yev%d" % ysl_, yev_st[ysl_])
                            ins = E.tensor_copy(
                                out=y_ev[:, ysl_ * 512:(ysl_ + 1) * 512],
                                in_=dn_ps) if VE else None
                            v_ye = w.fire(ins, "v", 1)
                            yev_v[ysl_] = v_ye
                            bank_set(dbank, "v", v_ye)
                            w.wait("scalar", "v", v_ye)
                            ins = E.dma_start(
                                out=yd[c][rowbase + s_ * 128: rowbase + (s_ + 1) * 128,
                                          q * 512:(q + 1) * 512],
                                in_=y_ev[:, ysl_ * 512:(ysl_ + 1) * 512]) if SC else None
                            yev_st[ysl_] = w.fire(ins, "yev%d" % ysl_, 16)
                            didx += 1
                    dwq_v.clear()

                yst_all = (w.n["yev0"], w.n["yev1"])

                for t in range(TT):
                    sl = t % 2
                    xsl = tb["xld"][:, sl * D:(sl + 1) * D]
                    for (s_, val) in ls["xld_gate"][sl]:
                        w.wait("sync", s_, val)
                    for (s_, val) in xos_gate[sl]:
                        w.wait("sync", s_, val)
                    ins = E.dma_start(out=xsl, in_=in_ap[c * CH + t * 128:
                                                         c * CH + (t + 1) * 128, :]) if SY else None
                    v_xo = w.fire(ins, "xld%d" % sl, 16)
                    ysl, y1s, y2s = (3 * t) % 4, (3 * t + 1) % 4, (3 * t + 2) % 4
                    for (s_, val) in ycb_gate[ysl]:
                        w.wait("sync", s_, val)
                    for i in range(2):
                        w.wait("sync", "yev%d" % i, yst_all[i])
                    ins = E.dma_start(out=ycb[:, ysl * D:(ysl + 1) * D],
                                      in_=yd[c][8 * C + t * 128: 8 * C + (t + 1) * 128, :]) if SY else None
                    v_ysh = w.fire(ins, "xin%d" % ysl, 16)
                    for (s_, val) in ycb_gate[y1s]:
                        w.wait("gpsimd", s_, val)
                    for i in range(2):
                        w.wait("gpsimd", "yev%d" % i, yst_all[i])
                    w.wait("gpsimd", "v", v_idx)
                    ins = E.indirect_dma_start(
                        out=ycb[:, y1s * D:(y1s + 1) * D], out_offset=None,
                        in_=yd[c],
                        in_offset=IOffs(ap=slotu[:, 2 * t:2 * t + 1], axis=0)) if GP else None
                    v_y1 = w.fire(ins, "yc%d" % y1s, 16)
                    for (s_, val) in ycb_gate[y2s]:
                        w.wait("gpsimd", s_, val)
                    ins = E.indirect_dma_start(
                        out=ycb[:, y2s * D:(y2s + 1) * D], out_offset=None,
                        in_=yd[c],
                        in_offset=IOffs(ap=slotu[:, 2 * t + 1:2 * t + 2], axis=0)) if GP else None
                    v_y2 = w.fire(ins, "yc%d" % y2s, 16)

                    w.wait("vector", "xld%d" % sl, v_xo)
                    w.wait("vector", "xin%d" % ysl, v_ysh)
                    ins = E.scalar_tensor_tensor(out=xsl, in0=ycb[:, ysl * D:(ysl + 1) * D],
                                                 scalar=0.5, in1=xsl,
                                                 op0=ALU.mult, op1=ALU.add) if VE else None
                    v_c1 = w.fire(ins, "v", 1)
                    w.wait("vector", "v", v_c1)
                    w.wait("vector", "yc%d" % y1s, v_y1)
                    ins = E.scalar_tensor_tensor(out=xsl, in0=ycb[:, y1s * D:(y1s + 1) * D],
                                                 scalar=w1p[:, 2 * t:2 * t + 1], in1=xsl,
                                                 op0=ALU.mult, op1=ALU.add) if VE else None
                    v_c2 = w.fire(ins, "v", 1)
                    w.wait("vector", "v", v_c2)
                    w.wait("vector", "yc%d" % y2s, v_y2)
                    ins = E.scalar_tensor_tensor(out=xsl, in0=ycb[:, y2s * D:(y2s + 1) * D],
                                                 scalar=w1p[:, 2 * t + 1:2 * t + 2], in1=xsl,
                                                 op0=ALU.mult, op1=ALU.add) if VE else None
                    v_c3 = w.fire(ins, "v", 1)
                    ycb_gate[ysl] = [("v", v_c1)]
                    ycb_gate[y1s] = [("v", v_c2)]
                    ycb_gate[y2s] = [("v", v_c3)]
                    w.wait("scalar", "v", v_c3)
                    ins = E.dma_start(out=out_ap[c * CH + t * 128: c * CH + (t + 1) * 128, :],
                                      in_=xsl) if SC else None
                    v_st = w.fire(ins, "xos%d" % sl, 16)
                    xos_gate[sl] = [("xos%d" % sl, v_st)]
                    ls["xld_gate"][sl] = [("xos%d" % sl, v_st)]
                if fill_cache:
                    cache_ready = ([("wgc%d" % i, w.n["wgc%d" % i]) for i in range(3)]
                                   + [("wdc%d" % i, w.n["wdc%d" % i]) for i in range(3)])

            phase_free(key, first_walk)

        def attn_a():
            tb = phase_buffers("attnA", lambda ps: dict(
                pst=ps.enter_context(nc.psum_tensor("Apst", [128, 2048], BF16)),
                xld=ps.enter_context(nc.sbuf_tensor("Axld", [128, 2 * D], F32, side="right")),
                xnb=ps.enter_context(nc.sbuf_tensor("Axnb", [128, 2 * D], BF16, side="right")),
                xnT=ps.enter_context(nc.sbuf_tensor("AxnT", [128, KT * CH], BF16, side="right")),
                qT=ps.enter_context(nc.sbuf_tensor("AqT", [128, KT * CH], BF16, side="right")),
                wqkv=ps.enter_context(nc.sbuf_tensor("Awqkv", [128, 6 * D], BF16, side="right")),
                katt=ps.enter_context(nc.sbuf_tensor("Akatt", [128, CH], F32, side="right")),
                kv=ps.enter_context(nc.sbuf_tensor("Akv", [128, CH], F32, side="right")),
                ones_s=ps.enter_context(nc.sbuf_tensor("ones_s", [128, CH], F32, side="right")),
                kvt=ps.enter_context(nc.sbuf_tensor("Akvt", [128, 2 * CH], F32, side="right")),
            ))
            tb["store_xnd"] = False
            pst = tb["pst"]
            ls = fresh_ln_state()
            qT, wqkv = tb["qT"], tb["wqkv"]
            ones_s = tb["ones_s"]
            ins = E.memset(ones_s[:], 1.0) if GP else None
            v_ones = w.fire(ins, "misc", 1)
            w.wait("vector", "misc", v_ones)
            kvst_n = [0, 0]
            qst_prev = 0
            v_car_last = 0
            wqkv_slot_p = {m: [0, 0] for m in range(3)}

            for c in range(_NCH):
                v_xnT_all = ln_stage(tb, ls, c, x1d, False, pst)
                for g in range(16):
                    slot = g % 2
                    wv_v = {}
                    for m in range(3):
                        w.wait("gpsimd", "p", wqkv_slot_p[m][slot])
                        ins = E.dma_start(
                            out=wqkv[:, (m * 2 + slot) * D:(m * 2 + slot + 1) * D
                                     ].rearrange("p (k n) -> p k n", k=16),
                            in_=wtd[m][:, g]) if GP else None
                        wv_v[m] = w.fire(ins, "w%s%d" % ("qkv"[m], slot), 16)

                    regions = [ps_gu[:, 0:1024], ps_gu[:, 1024:2048], ps_dn[:, 0:1024]]
                    rbanks = [[0, 1], [2, 3], [4, 5]]
                    p_qkv = {}
                    for m in range(3):
                        w.wait("tensor", "w%s%d" % ("qkv"[m], slot), wv_v[m])
                        if g == 0 and m == 0:
                            w.wait("tensor", "v", v_xnT_all)
                        bank_wait(rbanks[m])
                        last = None
                        for k in range(KT):
                            for nh in range(2):
                                if TE:
                                    last = E.matmul(
                                        regions[m][:, nh * 512:(nh + 1) * 512],
                                        wqkv[:, (m * 2 + slot) * D + k * 128:
                                             (m * 2 + slot) * D + (k + 1) * 128],
                                        tb["xnT"][:, kCH(k) + nh * 512:
                                                  kCH(k) + (nh + 1) * 512],
                                        start=(k == 0), stop=(k == KT - 1))
                        p_qkv[m] = w.fire(last, "p", 1)
                        wqkv_slot_p[m][slot] = p_qkv[m]

                    w.wait("vector", "p", p_qkv[0])
                    w.wait("vector", "qst", qst_prev)
                    ins = E.tensor_copy(out=qT[:, g * CH:(g + 1) * CH],
                                        in_=ps_gu[:, 0:1024]) if VE else None
                    v_q = w.fire(ins, "v", 1)
                    bank_set([0, 1], "v", v_q)
                    w.wait("vector", "p", p_qkv[1])
                    ins = E.tensor_copy(out=tb["katt"][:], in_=ps_gu[:, 1024:2048]) if VE else None
                    v_k = w.fire(ins, "v", 1)
                    bank_set([2, 3], "v", v_k)
                    w.wait("vector", "p", p_qkv[2])
                    w.wait("vector", "v", v_k)
                    ins = E.tensor_mul(tb["kv"][:], tb["katt"][:],
                                       ps_dn[:, 0:1024]) if VE else None
                    v_kv = w.fire(ins, "v", 1)
                    bank_set([4, 5], "v", v_kv)
                    w.wait("vector", "kvst%d" % slot, kvst_n[slot])
                    w.wait("vector", "v", v_kv)
                    ins = E.tensor_tensor_scan(
                        out=tb["kvt"][:, slot * CH:(slot + 1) * CH],
                        data0=ones_s[:], data1=tb["kv"][:],
                        initial=carry[:, g:g + 1],
                        op0=ALU.mult, op1=ALU.add) if VE else None
                    v_sc = w.fire(ins, "v", 1)
                    w.wait("vector", "v", v_sc)
                    ins = E.tensor_copy(
                        out=carry[:, g:g + 1],
                        in_=tb["kvt"][:, slot * CH + CH - 1: slot * CH + CH]) if VE else None
                    v_cr = w.fire(ins, "v", 1)
                    w.wait("vector", "v", v_cr)
                    v_car_last = v_cr
                    w.wait("scalar", "v", v_sc)
                    ins = E.dma_start(out=kvtd[g][:, c * CH:(c + 1) * CH],
                                      in_=tb["kvt"][:, slot * CH:(slot + 1) * CH]) if SC else None
                    kvst_n[slot] = w.fire(ins, "kvst%d" % slot, 16)

                w.wait("scalar", "v", w.n["v"])
                ins = E.dma_start(out=qTd[c], in_=qT[:]) if SC else None
                qst_prev = w.fire(ins, "qst", 16)

            w.wait("scalar", "v", v_car_last)
            ins = E.dma_start(out=cc_in, in_=carry[:]) if SC else None
            v_ci = w.fire(ins, "ccin", 16)
            w.wait("gpsimd", "ccin", v_ci)
            ins = E.collective_compute(
                "AllGather", ALU.bypass,
                replica_groups=[[0, 1], [2, 3], [4, 5], [6, 7]],
                ins=[cc_in.opt()], outs=[cc_out.opt()]) if GP else None
            v_cc = w.fire(ins, "cc", 1)
            w.wait("gpsimd", "cc", v_cc)
            ins = E.dma_start(out=Tpart[:], in_=cc_out[0:128, :]) if GP else None
            v_co = w.fire(ins, "ccout", 16)
            w.wait("vector", "ccout", v_co)
            ins = E.tensor_scalar(out=Spart[:], in0=Tpart[:], scalar1=mask_sb[:, 0:1],
                                  scalar2=None, op0=ALU.mult) if VE else None
            v_S = w.fire(ins, "v", 1)
            w.wait("vector", "v", v_S)
            phase_free("attnA", first_walk)
            return (w.n["kvst0"], w.n["kvst1"], w.n["qst"])

        def attn_b(kvst_totals):
            tb = phase_buffers("attnB", lambda ps: dict(
                ps_b2=ps.enter_context(nc.psum_tensor("Bps2", [128, 1024], F32)),
                qT=ps.enter_context(nc.sbuf_tensor("BqT", [128, 2 * KT * CH], BF16, side="right")),
                kvt=ps.enter_context(nc.sbuf_tensor("Bkvt", [128, KT * CH], F32, side="right")),
                wo=ps.enter_context(nc.sbuf_tensor("Bwo", [128, 2 * D], BF16, side="right")),
                xb=ps.enter_context(nc.sbuf_tensor("Bxb", [128, 4 * D], F32, side="right")),
            ))
            ps_b2 = tb["ps_b2"]
            qTb, kvtC, wob, xb = tb["qT"], tb["kvt"], tb["wo"], tb["xb"]
            v_vh_prev = 0
            wo_slot_p = [0, 0]
            qT_slot_p = [0, 0]
            xb_gate = {i: [] for i in range(4)}
            regionsA = [ps_gu[:, j * 512:(j + 1) * 512] for j in range(4)]
            banksA = [[0], [1], [2], [3]]
            regionsB = [ps_dn[:, 0:512], ps_dn[:, 512:1024],
                        ps_b2[:, 0:512], ps_b2[:, 512:1024]]
            banksB = [[4], [5], [6], [7]]
            for c in range(_NCH):
                if c == 0:
                    w.wait("sync", "kvst0", kvst_totals[0])
                    w.wait("sync", "kvst1", kvst_totals[1])
                w.wait("sync", "v", v_vh_prev)
                w.wait("gpsimd", "v", v_vh_prev)
                v_kvld = 0
                v_kvld2 = 0
                for g in range(16):
                    if g % 2 == 0:
                        ins = E.dma_start(out=kvtC[:, g * CH:(g + 1) * CH],
                                          in_=kvtd[g][:, c * CH:(c + 1) * CH]) if SY else None
                        v_kvld = w.fire(ins, "kvld", 16)
                    else:
                        ins = E.dma_start(out=kvtC[:, g * CH:(g + 1) * CH],
                                          in_=kvtd[g][:, c * CH:(c + 1) * CH]) if GP else None
                        v_kvld2 = w.fire(ins, "kvld2", 16)
                if c == 0:
                    w.wait("sync", "qst", kvst_totals[2])
                qsl = c % 2
                qS = qTb[:, qsl * KT * CH:(qsl + 1) * KT * CH]
                w.wait("sync", "p", qT_slot_p[qsl])
                ins = E.dma_start(out=qS, in_=qTd[c]) if SY else None
                v_qld = w.fire(ins, "qld", 16)

                v_vh = 0
                for g in range(16):
                    w.wait("vector", "kvld", v_kvld)
                    w.wait("vector", "kvld2", v_kvld2)
                    w.wait("vector", "qld", v_qld)
                    w.wait("vector", "p", qT_slot_p[qsl])
                    ins = E.scalar_tensor_tensor(
                        out=qS[:, g * CH:(g + 1) * CH],
                        in0=kvtC[:, g * CH:(g + 1) * CH],
                        scalar=Spart[:, g:g + 1],
                        in1=qS[:, g * CH:(g + 1) * CH],
                        op0=ALU.add, op1=ALU.mult) if VE else None
                    v_vh = w.fire(ins, "v", 1)
                v_vh_prev = v_vh

                for tp in range(TT // 2):
                    p_last = 0
                    v_xin = {}
                    for ti in range(2):
                        t = tp * 2 + ti
                        xslot = t % 4
                        for (s_, val) in xb_gate[xslot]:
                            w.wait("sync", s_, val)
                        ins = E.dma_start(out=xb[:, xslot * D:(xslot + 1) * D],
                                          in_=x1d[c * CH + t * 128: c * CH + (t + 1) * 128, :]) if SY else None
                        v_xin[ti] = w.fire(ins, "xin%d" % xslot, 16)
                    for g in range(16):
                        slot = g % 2
                        w.wait("gpsimd", "p", wo_slot_p[slot])
                        ins = E.dma_start(
                            out=wob[:, slot * D:(slot + 1) * D].rearrange(
                                "p (k n) -> p k n", k=16),
                            in_=wod[:, g]) if GP else None
                        v_wo = w.fire(ins, "wo%d" % slot, 16)
                        w.wait("tensor", "wo%d" % slot, v_wo)
                        if g == 0:
                            w.wait("tensor", "v", v_vh)
                            bank_wait(range(0, 8))
                        last = None
                        for ti in range(2):
                            t = tp * 2 + ti
                            regions = regionsA if ti == 0 else regionsB
                            for nh in range(4):
                                if TE:
                                    last = E.matmul(
                                        regions[nh],
                                        qS[:, g * CH + t * 128: g * CH + (t + 1) * 128],
                                        wob[:, slot * D + nh * 512: slot * D + (nh + 1) * 512],
                                        start=(g == 0), stop=(g == 15))
                        p_last = w.fire(last, "p", 1)
                        wo_slot_p[slot] = p_last
                    for ti in range(2):
                        t = tp * 2 + ti
                        xslot = t % 4
                        regions = regionsA if ti == 0 else regionsB
                        banks = banksA if ti == 0 else banksB
                        w.wait("vector", "p", p_last)
                        w.wait("vector", "xin%d" % xslot, v_xin[ti])
                        v_e = 0
                        for nh in range(4):
                            ins = E.tensor_add(
                                xb[:, xslot * D + nh * 512: xslot * D + (nh + 1) * 512],
                                xb[:, xslot * D + nh * 512: xslot * D + (nh + 1) * 512],
                                regions[nh]) if VE else None
                            v_e = w.fire(ins, "v", 1)
                            bank_set(banks[nh], "v", v_e)
                        w.wait("scalar", "v", v_e)
                        ins = E.dma_start(out=x2d[c * CH + t * 128: c * CH + (t + 1) * 128, :],
                                          in_=xb[:, xslot * D:(xslot + 1) * D]) if SC else None
                        st_v = w.fire(ins, "xst%d" % xslot, 16)
                        xb_gate[xslot] = [("xst%d" % xslot, st_v)]
                qT_slot_p[qsl] = p_last
            phase_free("attnB", first_walk)

        ffn_phase("f1", x0, x1d)
        barrier()
        kq = attn_a()
        barrier()
        attn_b(kq)
        barrier()
        ffn_phase("f3", x2d, out_e)

        w.wait("gpsimd", "xos0", w.n["xos0"])
        w.wait("gpsimd", "xos1", w.n["xos1"])

    with nc.Block() as block:
        state = {"first": True}

        def mk(name):
            def _run(eng):
                schedule(Walk(name, eng, sems), state["first"])
                state["first"] = False
            return _run

        block.sync(mk("sync"))
        block.gpsimd(mk("gpsimd"))
        block.tensor(mk("tensor"))
        block.scalar(mk("scalar"))
        block.vector(mk("vector"))

    st.close()
    return nc


def kernel(**inputs):
    import ml_dtypes
    x = np.ascontiguousarray(np.asarray(inputs["L0_attn_x"], dtype=np.float32))
    flat = x.reshape(8, 4096, 2048)
    names = {
        "Wq": "L0_attn_Wq", "Wk": "L0_attn_Wk", "Wv": "L0_attn_Wv", "Wo": "L0_attn_Wo",
        "router_w": "L0_ffn_router_w", "gate_w": "L0_ffn_gate_w",
        "up_w": "L0_ffn_up_w", "down_w": "L0_ffn_down_w",
        "s_gate_w": "L0_ffn_s_gate_w", "s_gate_b": "L0_ffn_s_gate_b",
        "s_up_w": "L0_ffn_s_up_w", "s_up_b": "L0_ffn_s_up_b",
        "s_down_w": "L0_ffn_s_down_w", "s_down_b": "L0_ffn_s_down_b",
    }
    weights = {k: np.ascontiguousarray(np.asarray(inputs[v], dtype=np.float32))
               for k, v in names.items()}
    idf = np.eye(128, dtype=np.float32)
    idfb = np.eye(128, dtype=np.float32).astype(ml_dtypes.bfloat16)
    triU = np.triu(np.ones((128, 128), dtype=np.float32), 1)
    iotaF = np.tile(np.arange(128, dtype=np.float32), (128, 1))
    iota2 = np.zeros((128, 2 * TT), dtype=np.float32)
    for t in range(TT):
        iota2[:, 2 * t] = np.arange(128)
        iota2[:, 2 * t + 1] = 128 * t
    iota2 = np.ascontiguousarray(iota2.astype(ml_dtypes.bfloat16))
    eCb = np.tile((np.arange(8, dtype=np.float32) * C)[None, :], (128, 1))
    in_maps = []
    for c in range(8):
        m = dict(weights)
        m["x0"] = np.ascontiguousarray(flat[c])
        m["idf"] = idf
        m["idfb"] = idfb
        m["triU"] = np.ascontiguousarray(triU)
        m["iotaF"] = np.ascontiguousarray(iotaF)
        m["iota2"] = iota2
        m["eCb"] = np.ascontiguousarray(eCb)
        m["maskv"] = np.full((128, 1), 1.0 if c % 2 == 1 else 0.0, dtype=np.float32)
        in_maps.append(m)
    nc = build_nc()
    res = run_bass_kernel_spmd(nc, in_maps, core_ids=list(range(8)))
    out = np.concatenate([r["out"] for r in res.results], axis=0)
    return out.reshape(4, 8192, 2048)
